# version 19
# baseline (speedup 1.0000x reference)
import contextlib
import numpy as np
import concourse.bass as bass
import concourse.mybir as mybir
from concourse.bass_utils import run_bass_kernel_spmd

F32 = mybir.dt.float32
AF = mybir.ActivationFunctionType
ALU = mybir.AluOpType
AX = mybir.AxisListType

NCORE = 8
DM = 1024
SEQ = 4096
SEG = 2048
NSEG = 2
NT = 16
KC = 8
NPAIR = 8
DH = 64
NLEV = 7
LVL_LEN = [2048, 1024, 512, 256, 128, 64, 32]
NE = 16
EH = 256
HSH = 2048
BIG = 1.0e30
MASKNEG = -1.0e9
DEBUG_TAPS = False
TAP_LIST = None
EARLY_POOL = True
PERTILE = 2
SKIP_P2 = False
SKIP_P3 = False
SKIP_P4 = False
SKIP_P5 = False

ENGINES = ("tensor", "vector", "scalar", "gpsimd", "sync")


def dap(apobj, offset, dims):
    return bass.AP(apobj.tensor, offset, dims)


def sap(handle, offset, dims):
    return bass.AP(handle, offset, dims)


class Sched:
    def __init__(self, nc, semnames):
        self.nc = nc
        self.semnames = list(semnames)
        self.sems = {}
        self.cnt = {k: 0 for k in semnames}
        self.ops = {e: [] for e in ENGINES}
        self.lastw = {e: {k: 0 for k in semnames} for e in ENGINES}

    def op(self, eng, fn, waits=(), inc=None, n=1):
        if eng == "vector":
            waits = list(waits) + [("sDVE", self.cnt["sDVE"])]
            if inc is None and fn is not None:
                inc = "sDVE"
        wl = []
        for s, v in waits:
            if v > 0 and v > self.lastw[eng][s]:
                wl.append((s, v))
                self.lastw[eng][s] = v
        self.ops[eng].append((fn, wl, inc, n))
        if inc is not None:
            self.cnt[inc] += n
            return self.cnt[inc]
        return None

    def now(self, *names):
        return [(s, self.cnt[s]) for s in names]

    def fence_engine(self, eng):
        for s in self.semnames:
            v = self.cnt[s]
            if v > self.lastw[eng][s]:
                self.ops[eng].append((None, [(s, v)], None, 0))
                self.lastw[eng][s] = v

    def emit_block(self):
        ops = self.ops
        self.ops = {e: [] for e in ENGINES}
        sems = self.sems
        with self.nc.Block() as blk:
            for eng in ENGINES:
                if not ops[eng]:
                    continue

                def body(ee, _lst=ops[eng]):
                    for fn, wl, inc, n in _lst:
                        for s, v in wl:
                            ee.wait_ge(sems[s], v)
                        if fn is None:
                            continue
                        ins = fn(ee)
                        if inc is not None:
                            ins.then_inc(sems[inc], n)

                getattr(blk, eng)(body)


class DmaStage:
    def __init__(self, sched, eng, sem):
        self.s = sched
        self.eng = eng
        self.sem = sem

    def dma(self, out, in_, waits=(), slow=False):
        if slow:
            fn = lambda e, o=out, i=in_: e.dma_start(
                o, i, allow_slow_non_contiguous=True)
        else:
            fn = lambda e, o=out, i=in_: e.dma_start(o, i)
        self.s.op(self.eng, fn, waits=waits, inc=self.sem, n=16)

    def wait(self):
        return (self.sem, self.s.cnt[self.sem])


class DmaPool:
    def __init__(self, sched, eng, semnames):
        self.s = sched
        self.eng = eng
        self.names = list(semnames)
        self.i = 0

    def stage(self):
        sem = self.names[self.i % len(self.names)]
        self.i += 1
        return DmaStage(self.s, self.eng, sem)


def ln_tiles(S, x_res, lnf, ln_tm, stat, ident, pb, tile_extra_waits,
             after_tile=None):
    hist = {"pe_tr": [0, 0], "act_sq": [0, 0]}
    dve_t = []
    act_t = []
    for t in range(NT):
        par = 0
        so = 0
        lo = 0
        xt = x_res[:, t * 1024:(t + 1) * 1024]
        war = [("sPE", hist["pe_tr"][par]), ("sACT", hist["act_sq"][par])]
        S.op("vector", lambda e, xt=xt, so=so: e.reduce_sum(
            stat[:, so:so + 1], xt, AX.X),
            waits=tile_extra_waits(t) + war, inc="sDVE")
        S.op("vector", lambda e, so=so: e.tensor_scalar(
            stat[:, so + 1:so + 2], stat[:, so:so + 1], -1.0 / DM, None,
            op0=ALU.mult), inc="sDVE")
        dve_m = S.cnt["sDVE"]
        S.op("scalar", lambda e, xt=xt, so=so, lo=lo: e.activation(
            ln_tm[:, lo:lo + 1024], xt, AF.Square, bias=stat[:, so + 1:so + 2],
            accum_out=stat[:, so + 2:so + 3]),
            waits=[("sDVE", dve_m), ("sPE", hist["pe_tr"][par])], inc="sACT")
        act_sq = S.cnt["sACT"]
        hist["act_sq"][par] = act_sq
        S.op("vector", lambda e, so=so: e.tensor_scalar(
            stat[:, so + 3:so + 4], stat[:, so + 2:so + 3], 1.0 / DM, 1e-5,
            op0=ALU.mult, op1=ALU.add),
            waits=[("sACT", act_sq)], inc="sDVE")
        dve_v = S.cnt["sDVE"]
        S.op("scalar", lambda e, so=so: e.activation(
            stat[:, so + 4:so + 5], stat[:, so + 3:so + 4], AF.Sqrt),
            waits=[("sDVE", dve_v)], inc="sACT")
        act_rt = S.cnt["sACT"]
        S.op("vector", lambda e, so=so: e.reciprocal(
            stat[:, so + 5:so + 6], stat[:, so + 4:so + 5]),
            waits=[("sACT", act_rt)], inc="sDVE")
        S.op("vector", lambda e, xt=xt, so=so, lo=lo: e.tensor_scalar(
            ln_tm[:, lo:lo + 1024], xt, stat[:, so + 1:so + 2],
            stat[:, so + 5:so + 6], op0=ALU.add, op1=ALU.mult),
            waits=[("sACT", act_sq)], inc="sDVE")
        dve_n = S.cnt["sDVE"]
        for half in range(2):
            for j in range(4):
                kc = half * 4 + j
                S.op("tensor", lambda e, kc=kc, j=j, lo=lo: e.transpose(
                    pb[6][:, j * 128:j * 128 + 128],
                    ln_tm[:, lo + kc * 128:lo + kc * 128 + 128], ident[:, :]),
                    waits=[("sDVE", max(dve_n, S.cnt["sDVE"])),
                           ("sACT", S.cnt["sACT"])],
                    inc="sPE")
            pe_tr = S.cnt["sPE"]
            dst = sap(lnf, (half * 4) * SEG + t * 128,
                      [[int(lnf.shape[1]), 128], [SEG, 4], [1, 128]])
            if half == 0:
                S.op("vector", lambda e, dst=dst: e.tensor_copy(
                    dst, pb[6][:, 0:512]), waits=[("sPE", pe_tr)], inc="sDVE")
            else:
                S.op("scalar", lambda e, dst=dst: e.copy(
                    dst, pb[6][:, 0:512]), waits=[("sPE", pe_tr)], inc="sACT")
        hist["pe_tr"][par] = S.cnt["sPE"]
        dve_t.append(S.cnt["sDVE"])
        act_t.append(S.cnt["sACT"])
        if after_tile is not None:
            after_tile(t)
    return dve_t, act_t


def build_P2(S, spool, seg, wq, wk, wv, wq_sl, wk_sl, wv_sl, lnf,
             qT, kT, vT, Yac, S_sb, A_sb, nm_sb, x_res,
             ident, mask0, msks, poolA, poolB, bq_c, bk_c, bv_c, pb):
    nvT = int(vT.shape[1])

    S.op("gpsimd", lambda e: e.memset(
        sap(vT, 64, [[nvT, 128], [130, 24], [65, 2]]), 1.0), inc="sPOOL")
    ones_done = S.cnt["sPOOL"]

    wstage = [None] * (NPAIR + 1)
    wstage[0] = spool.stage()
    for wsl, wdr in ((wq_sl, wq), (wk_sl, wk), (wv_sl, wv)):
        wstage[0].dma(wsl[:, :],
                      dap(wdr, 0, [[DM, 128], [128 * DM, 8], [1, 128]]))

    lnf_ready = S.now("sDVE", "sACT")
    evac_act = {}
    pair_dve = 0
    pair_act = 0
    fin_pend = [None]

    def emit_finalize(p_, yacc_dve_):
        for h in range(2):
            for tb in range(4):
                for i in range(4):
                    t = tb * 4 + i
                    S.op("tensor", lambda e, h=h, t=t, i=i: e.transpose(
                        pb[1][:, i * 128:i * 128 + 128],
                        Yac[:, h * 2048 + t * 128:h * 2048 + t * 128 + 128],
                        ident[:, :]),
                        waits=[("sDVE", max(yacc_dve_, S.cnt["sDVE"])),
                               ("sACT", S.cnt["sACT"])], inc="sPE")
                pe_f = S.cnt["sPE"]
                rcol = 16 + (tb % 2) * 4
                S.op("vector", lambda e, rcol=rcol: e.tensor_scalar(
                    nm_sb[:, rcol:rcol + 4],
                    sap(pb[1], 64, [[512, 128], [128, 4]]),
                    1e-8, None, op0=ALU.add),
                    waits=[("sPE", pe_f)], inc="sDVE")
                S.op("vector", lambda e, rcol=rcol: e.reciprocal(
                    nm_sb[:, rcol:rcol + 4], nm_sb[:, rcol:rcol + 4]),
                    inc="sDVE")
                for i in range(4):
                    t = tb * 4 + i
                    pcol = t * 1024 + (p_ * 2 + h) * 64
                    S.op("vector", lambda e, i=i, pcol=pcol, rcol=rcol:
                         e.scalar_tensor_tensor(
                             x_res[:, pcol:pcol + 64],
                             pb[1][:, i * 128:i * 128 + 64],
                             nm_sb[:, rcol + i:rcol + i + 1],
                             x_res[:, pcol:pcol + 64],
                             op0=ALU.mult, op1=ALU.add), inc="sDVE")

    for p in range(NPAIR):
        ww = wstage[p].wait()
        for g in range(4):
            gp = g % 2
            for (wsl, bcol, dstbuf, bank) in (
                    (wq_sl, bq_c, qT, 0 + 2 * gp), (wk_sl, bk_c, kT, 1 + 2 * gp)):
                for kc in range(KC):
                    S.op("tensor", lambda e, wsl=wsl, kc=kc, g=g, bank=bank:
                         e.matmul(
                             pb[bank][:, 0:512],
                             wsl[:, kc * 128:kc * 128 + 128],
                             lnf[:, kc * SEG + g * 512:kc * SEG + g * 512 + 512],
                             start=(kc == 0), stop=(kc == KC - 1)),
                         waits=[ww, ("sDVE", pair_dve)] + lnf_ready +
                         [("sACT", max(evac_act.get((bank,), 0), pair_act))],
                         inc="sPE")
                pe_mm = S.cnt["sPE"]
                S.op("scalar", lambda e, dstbuf=dstbuf, g=g, bank=bank,
                     bcol=bcol, p=p: e.activation(
                         dstbuf[:, g * 512:g * 512 + 512], pb[bank][:, 0:512],
                         AF.Identity, bias=bcol[:, p:p + 1]),
                     waits=[("sPE", pe_mm)], inc="sACT")
                evac_act[(bank,)] = S.cnt["sACT"]
            vbank = 4 + gp
            for kc in range(KC):
                S.op("tensor", lambda e, kc=kc, g=g, vbank=vbank: e.matmul(
                    pb[vbank][:, 0:512], wv_sl[:, kc * 128:kc * 128 + 128],
                    lnf[:, kc * SEG + g * 512:kc * SEG + g * 512 + 512],
                    start=(kc == 0), stop=(kc == KC - 1)),
                    waits=[ww, ("sDVE", pair_dve)] +
                    [("sACT", max(evac_act.get((vbank,), 0), pair_act))],
                    inc="sPE")
            pe_v = S.cnt["sPE"]
            S.op("scalar", lambda e, p=p, gp=gp, vbank=vbank: e.activation(
                S_sb[:, gp * 512:gp * 512 + 512], pb[vbank][:, 0:512],
                AF.Identity, bias=bv_c[:, p:p + 1]),
                waits=[("sPE", pe_v)] + S.now("sDVE"),
                inc="sACT")
            act_v = S.cnt["sACT"]
            evac_act[(vbank,)] = act_v
            tb_ = 6 + gp
            for i in range(4):
                S.op("tensor", lambda e, i=i, gp=gp, tb_=tb_: e.transpose(
                    pb[tb_][:, i * 128:i * 128 + 128],
                    S_sb[:, gp * 512 + i * 128:gp * 512 + i * 128 + 128],
                    ident[:, :]),
                    waits=[("sACT", act_v),
                           ("sDVE", evac_act.get((6, gp), 0))], inc="sPE")
            pe_tr = S.cnt["sPE"]
            dst = sap(vT, g * 4 * 130, [[nvT, 128], [130, 4], [65, 2], [1, 64]])
            srcp = sap(pb[tb_], 0, [[512, 128], [128, 4], [64, 2], [1, 64]])
            S.op("vector", lambda e, dst=dst, srcp=srcp: e.tensor_copy(dst, srcp),
                 waits=[("sPE", pe_tr), ("sPOOL", ones_done)], inc="sDVE")
            evac_act[(6, gp)] = S.cnt["sDVE"]
            evac_act[(4, gp)] = S.cnt["sDVE"]
        qkv_pe = S.cnt["sPE"]
        qkv_act = S.cnt["sACT"]

        if p + 1 < NPAIR:
            wstage[p + 1] = spool.stage()
            for wsl, wdr in ((wq_sl, wq), (wk_sl, wk), (wv_sl, wv)):
                wstage[p + 1].dma(
                    wsl[:, :], dap(wdr, (p + 1) * 128,
                                   [[DM, 128], [128 * DM, 8], [1, 128]]),
                    waits=[("sPE", qkv_pe)])

        attend_hist = []
        prev = {"act_e": [0, 0, 0], "pe_y": [0, 0, 0],
                "act_e_i": [None, None, None],
                "dve_sst": [0, 0, 0], "dve_ya": [0, 0], "pb1_dve": 0}
        bglob = [0]
        pend = [None]

        def emit_S(lv, h, t0, nt_b, w, qoff, bp, ypar):
            sbank = (2, 3, 0)[bp]
            scale = 0.125 * (0.25 ** lv)
            maska = mask0 if lv == 0 else msks
            so_ = bp * 512
            pdims = [[512, w], [128, nt_b], [1, w]]
            sdims = [[1536, w], [128, nt_b], [1, w]]
            pe_s_i = []
            for i in range(nt_b):
                co = qoff + (t0 + i) * 128
                S.op("tensor", lambda e, h=h, co=co, w=w, i=i, sbank=sbank:
                     e.matmul(
                         pb[sbank][0:w, i * 128:i * 128 + w],
                         qT[h * 64:h * 64 + 64, co:co + w],
                         kT[h * 64:h * 64 + 64, co:co + w],
                         start=True, stop=True),
                     waits=[("sDVE", prev["dve_sst"][bp]),
                            ("sACT", qkv_act)],
                     inc="sPE")
                pe_s_i.append(S.cnt["sPE"])
            S.op("vector", lambda e, sbank=sbank, pdims=pdims, sdims=sdims,
                 w=w, nt_b=nt_b, so_=so_, scale=scale, maska=maska:
                 e.scalar_tensor_tensor(
                     sap(S_sb, so_, sdims), sap(pb[sbank], 0, pdims),
                     scale, sap(maska, 0, [[128, w], [0, nt_b], [1, w]]),
                     op0=ALU.mult, op1=ALU.add),
                 waits=[("sPE", pe_s_i[-1]),
                        ("sACT", prev["act_e"][bp])], inc="sDVE")
            prev["dve_sst"][bp] = S.cnt["sDVE"]
            S.op("vector", lambda e, sdims=sdims, w=w, nt_b=nt_b,
                 bp=bp, so_=so_: e.reduce_max(
                     nm_sb[0:w, bp * 4:bp * 4 + nt_b],
                     sap(S_sb, so_, sdims), AX.X, negate=True),
                 inc="sDVE")
            dve_mx = S.cnt["sDVE"]
            acc = []
            for i in range(nt_b):
                S.op("scalar", lambda e, i=i, w=w, so_=so_, bp=bp:
                     e.activation(
                         sap(A_sb, so_ + i * 128, [[1536, w], [1, w]]),
                         sap(S_sb, so_ + i * 128, [[1536, w], [1, w]]),
                         AF.Exp,
                         bias=nm_sb[0:w, bp * 4 + i:bp * 4 + i + 1]),
                     waits=[("sDVE", dve_mx), ("sPE", prev["pe_y"][bp])],
                     inc="sACT")
                acc.append(S.cnt["sACT"])
            prev["act_e"][bp] = S.cnt["sACT"]
            prev["act_e_i"][bp] = acc

        def emit_rest(lv, h, t0, nt_b, w, slot0, bp, ypar):
            so_ = bp * 512
            tbank = (6, 7, 1)[bp]
            ybank = 4 + ypar
            act_i = prev["act_e_i"][bp]
            pe_t_i = []
            pb1w = ([("sDVE", prev["pb1_dve"])] if tbank == 1 else [])
            for i in range(nt_b):
                S.op("tensor", lambda e, i=i, w=w, tbank=tbank, so_=so_:
                     e.transpose(
                         pb[tbank][0:w, i * 128:i * 128 + w],
                         sap(A_sb, so_ + i * 128, [[1536, w], [1, w]]),
                         ident[0:w, 0:w]),
                     waits=pb1w + [("sACT", act_i[i])], inc="sPE")
                pe_t_i.append(S.cnt["sPE"])
            S.op("scalar", lambda e, w=w, nt_b=nt_b, tbank=tbank,
                 so_=so_: e.copy(
                     sap(A_sb, so_, [[1536, w], [128, nt_b], [1, w]]),
                     sap(pb[tbank], 0, [[512, w], [128, nt_b], [1, w]])),
                 waits=[("sPE", pe_t_i[-1])], inc="sACT")
            cp_act = S.cnt["sACT"]
            for i in range(nt_b):
                slot = slot0 + t0 + i
                lhsT = sap(vT, slot * 130 + h * 65, [[nvT, w], [1, 65]])
                S.op("tensor", lambda e, lhsT=lhsT, i=i, w=w, so_=so_,
                     ybank=ybank: e.matmul(
                         pb[ybank][0:65, i * 128:i * 128 + w],
                         lhsT,
                         sap(A_sb, so_ + i * 128, [[1536, w], [1, w]]),
                         start=True, stop=True),
                     waits=[("sACT", cp_act),
                            ("sDVE", prev["dve_ya"][ypar]),
                            ("sPOOL", ones_done)], inc="sPE")
            pe_y = S.cnt["sPE"]
            prev["pe_y"][bp] = pe_y
            rep = 1 << lv
            for i in range(nt_b):
                span0 = h * 2048 + (t0 + i) * 128 * rep
                ydst = sap(Yac, span0, [[int(Yac.shape[1]), 65], [1, w * rep]])
                ysrc = sap(pb[ybank], i * 128, [[512, 65], [1, w], [0, rep]])
                if lv == 0:
                    S.op("vector", lambda e, ydst=ydst, ysrc=ysrc:
                         e.tensor_copy(ydst, ysrc),
                         waits=[("sPE", pe_y)], inc="sDVE")
                else:
                    S.op("vector", lambda e, ydst=ydst, ysrc=ysrc:
                         e.tensor_tensor(ydst, ydst, ysrc, ALU.add),
                         waits=[("sPE", pe_y)], inc="sDVE")
            prev["dve_ya"][ypar] = S.cnt["sDVE"]

        def flush():
            if pend[0] is not None:
                emit_rest(*pend[0])
                pend[0] = None

        qkv_vdve = S.cnt["sDVE"]
        qkadd_dve = [0] * (NLEV + 1)
        vpool_dve = [qkv_vdve] * NLEV

        for lv in range(NLEV):
            L = LVL_LEN[lv]
            ntl = (L + 127) // 128
            qoff = 0 if lv % 2 == 0 else 2048
            slot0 = 0 if lv % 2 == 0 else 16
            if lv > 0:
                flush()
                pslot0 = 0 if (lv - 1) % 2 == 0 else 16
                Lp = LVL_LEN[lv - 1]
                guard = ([("sPE", attend_hist[lv - 2])] if lv >= 2 else
                         [("sPE", qkv_pe)])
                if not EARLY_POOL:
                    psrc = 2048 - qoff
                    for buf in (qT, kT):
                        S.op("vector", lambda e, buf=buf, qoff=qoff,
                             psrc=psrc, L=L: e.tensor_tensor(
                                 buf[:, qoff:qoff + L],
                                 buf[:, psrc:psrc + 2 * L:2],
                                 buf[:, psrc + 1:psrc + 2 * L:2], ALU.add),
                             waits=guard, inc="sDVE")
                    qkadd_dve[lv] = S.cnt["sDVE"]
                n_src = (Lp + 127) // 128
                pool_cp = 0
                for j in range(ntl):
                    w_out = min(128, L - j * 128)
                    mats = []
                    for (pm, st_) in ((poolA, 2 * j), (poolB, 2 * j + 1)):
                        if st_ >= n_src:
                            continue
                        wi = min(128, Lp - st_ * 128)
                        if wi <= 0:
                            continue
                        mats.append((pm, st_, wi))
                    for mi, (pm, st_, wi) in enumerate(mats):
                        rhs = sap(vT, (pslot0 + st_) * 130,
                                  [[nvT, wi], [65, 2], [1, 64]])
                        if w_out < 128:
                            lhsT = pm[0:wi, 0:w_out]
                        else:
                            lhsT = pm[0:wi, :]
                        vw = (guard + [("sDVE", max(vpool_dve[lv - 1],
                                                    pool_cp,
                                                    prev["dve_sst"][2]))]
                              if EARLY_POOL else
                              guard + S.now("sACT", "sDVE"))
                        S.op("tensor", lambda e, lhsT=lhsT, rhs=rhs, mi=mi,
                             last=(mi == len(mats) - 1), w_out=w_out: e.matmul(
                                 pb[0][0:w_out, 0:128], lhsT, rhs,
                                 start=(mi == 0), stop=last),
                             waits=vw, inc="sPE")
                    pe_p = S.cnt["sPE"]
                    dstv = sap(vT, (slot0 + j) * 130,
                               [[nvT, w_out], [65, 2], [1, 64]])
                    srcv = sap(pb[0], 0, [[512, w_out], [64, 2], [1, 64]])
                    S.op("vector", lambda e, dstv=dstv, srcv=srcv:
                         e.tensor_copy(dstv, srcv),
                         waits=[("sPE", pe_p)], inc="sDVE")
                    pool_cp = S.cnt["sDVE"]
                vpool_dve[lv] = S.cnt["sDVE"]
                for ph in range(3):
                    prev["dve_sst"][ph] = max(prev["dve_sst"][ph],
                                              qkadd_dve[lv])
                prev["dve_sst"][2] = max(prev["dve_sst"][2], pool_cp)
            nbatch = (ntl + 3) // 4
            first = True
            for h in range(2):
                for b in range(nbatch):
                    t0 = b * 4
                    nt_b = min(4, ntl - t0)
                    w = min(128, L)
                    bp = bglob[0] % 3
                    ypar = bglob[0] % 2
                    bglob[0] += 1
                    emit_S(lv, h, t0, nt_b, w, qoff, bp, ypar)
                    this = (lv, h, t0, nt_b, w, slot0, bp, ypar)
                    if fin_pend[0] is not None:
                        emit_finalize(*fin_pend[0])
                        fin_pend[0] = None
                        prev["pb1_dve"] = S.cnt["sDVE"]
                    if pend[0] is not None:
                        emit_rest(*pend[0])
                    pend[0] = this
                    if first and EARLY_POOL and lv + 1 < NLEV:
                        first = False
                        Ln = LVL_LEN[lv + 1]
                        qoff_n = 2048 - qoff
                        gpe = (attend_hist[lv - 1] if lv >= 1 else qkv_pe)
                        for buf in (qT, kT):
                            S.op("vector", lambda e, buf=buf, qoff_n=qoff_n,
                                 qoff=qoff, Ln=Ln: e.tensor_tensor(
                                     buf[:, qoff_n:qoff_n + Ln],
                                     buf[:, qoff:qoff + 2 * Ln:2],
                                     buf[:, qoff + 1:qoff + 2 * Ln:2],
                                     ALU.add),
                                 waits=[("sPE", gpe), ("sACT", qkv_act)],
                                 inc="sDVE")
                        qkadd_dve[lv + 1] = S.cnt["sDVE"]
            attend_hist.append(S.cnt["sPE"])
        flush()
        fin_pend[0] = (p, S.cnt["sDVE"])
        pair_dve = S.cnt["sDVE"]
        pair_act = S.cnt["sACT"]

    emit_finalize(*fin_pend[0])



def build_P3(S, seg, x_res, lnf, ln_tm, stat, ident, rwT, cb_bc,
             scoreA, R1, R2, R3, R4, R5, pb):
    def after_tile(t):
        lnw = S.now("sDVE", "sACT")
        for kc in range(KC):
            S.op("tensor", lambda e, t=t, kc=kc: e.matmul(
                pb[2][:, t * 16:t * 16 + 16],
                lnf[:, kc * SEG + t * 128:kc * SEG + t * 128 + 128],
                rwT[:, kc * 16:kc * 16 + 16],
                start=(kc == 0), stop=(kc == KC - 1)),
                waits=lnw, inc="sPE")

    dve_t, act_t = ln_tiles(S, x_res, lnf, ln_tm, stat, ident, pb,
                            lambda t: [], after_tile)
    lnf_ready = S.now("sDVE", "sACT")
    pe_r = S.cnt["sPE"]
    S.op("scalar", lambda e: e.activation(
        scoreA[:, :], pb[2][:, 0:256], AF.Sigmoid),
        waits=[("sPE", pe_r)], inc="sACT")
    act_s = S.cnt["sACT"]

    def V(ten, off, dims):
        return sap(ten, off, [[int(ten.shape[1]), 128]] + dims)

    dve0 = S.now("sDVE")
    S.op("vector", lambda e: e.tensor_tensor(
        R1[:, :], V(scoreA, 0, [[16, 16], [1, 16]]),
        V(cb_bc, 0, [[0, 16], [1, 16]]), ALU.add),
        waits=[("sACT", act_s)] + dve0, inc="sDVE")
    S.op("vector", lambda e: e.tensor_tensor(
        V(R2, 0, [[8, 16], [2, 4], [1, 2]]),
        V(R1, 0, [[16, 16], [4, 4], [2, 2]]),
        V(R1, 1, [[16, 16], [4, 4], [2, 2]]), ALU.max), inc="sDVE")
    S.op("vector", lambda e: e.tensor_tensor(
        V(R2, 128, [[8, 16], [2, 4], [1, 2]]),
        V(R1, 0, [[16, 16], [4, 4], [2, 2]]),
        V(R1, 1, [[16, 16], [4, 4], [2, 2]]), ALU.min), inc="sDVE")
    S.op("vector", lambda e: e.tensor_tensor(
        V(R3, 0, [[4, 16], [1, 4]]),
        V(R2, 0, [[8, 16], [2, 4]]), V(R2, 1, [[8, 16], [2, 4]]), ALU.max),
        inc="sDVE")
    S.op("vector", lambda e: e.tensor_tensor(
        V(R3, 64, [[4, 16], [1, 4]]),
        V(R2, 0, [[8, 16], [2, 4]]), V(R2, 1, [[8, 16], [2, 4]]), ALU.min),
        inc="sDVE")
    S.op("vector", lambda e: e.tensor_tensor(
        V(R3, 128, [[4, 16], [1, 4]]),
        V(R2, 128, [[8, 16], [2, 4]]), V(R2, 129, [[8, 16], [2, 4]]), ALU.max),
        inc="sDVE")
    S.op("vector", lambda e: e.tensor_tensor(
        V(R3, 192, [[4, 16], [1, 4]]),
        V(R3, 64, [[4, 16], [1, 4]]), V(R3, 128, [[4, 16], [1, 4]]), ALU.max),
        inc="sDVE")
    S.op("vector", lambda e: e.tensor_tensor(
        V(R4, 0, [[4, 16], [1, 4]]),
        V(R3, 0, [[4, 16], [1, 4]]), V(R3, 192, [[4, 16], [1, 4]]), ALU.add),
        inc="sDVE")
    S.op("vector", lambda e: e.tensor_tensor(
        V(R4, 64, [[2, 16], [1, 2]]),
        V(R4, 0, [[4, 16], [2, 2]]), V(R4, 1, [[4, 16], [2, 2]]), ALU.max),
        inc="sDVE")
    S.op("vector", lambda e: e.tensor_tensor(
        V(R4, 96, [[2, 16], [1, 2]]),
        V(R4, 0, [[4, 16], [2, 2]]), V(R4, 1, [[4, 16], [2, 2]]), ALU.min),
        inc="sDVE")
    S.op("vector", lambda e: e.tensor_tensor(
        V(R4, 128, [[1, 16]]),
        V(R4, 64, [[2, 16]]), V(R4, 65, [[2, 16]]), ALU.min), inc="sDVE")
    S.op("vector", lambda e: e.tensor_tensor(
        V(R4, 144, [[1, 16]]),
        V(R4, 96, [[2, 16]]), V(R4, 97, [[2, 16]]), ALU.max), inc="sDVE")
    S.op("vector", lambda e: e.tensor_tensor(
        V(R4, 160, [[1, 16]]),
        V(R4, 128, [[1, 16]]), V(R4, 144, [[1, 16]]), ALU.max), inc="sDVE")
    S.op("vector", lambda e: e.tensor_tensor(
        V(R4, 192, [[4, 16], [1, 4]]),
        V(R4, 0, [[4, 16], [1, 4]]),
        V(R4, 160, [[1, 16], [0, 4]]), ALU.is_ge), inc="sDVE")
    S.op("vector", lambda e: e.tensor_scalar(
        R2[:, :], V(R4, 192, [[4, 16], [1, 4], [0, 4]]),
        1.0, BIG, op0=ALU.subtract, op1=ALU.mult), inc="sDVE")
    S.op("vector", lambda e: e.tensor_tensor(R5[:, :], R1[:, :], R2[:, :],
                                             ALU.add), inc="sDVE")
    S.op("vector", lambda e: e.tensor_copy(R3[:, :], R5[:, :]), inc="sDVE")
    for it in range(4):
        S.op("vector", lambda e: e.reduce_max(
            V(R4, 0, [[1, 16]]), V(R3, 0, [[16, 16], [1, 16]]), AX.X),
            inc="sDVE")
        if it < 3:
            S.op("vector", lambda e: e.tensor_tensor(
                R2[:, :], V(R3, 0, [[16, 16], [1, 16]]),
                V(R4, 0, [[1, 16], [0, 16]]), ALU.is_ge), inc="sDVE")
            S.op("vector", lambda e: e.scalar_tensor_tensor(
                R3[:, :], R2[:, :], -BIG, R3[:, :],
                op0=ALU.mult, op1=ALU.add), inc="sDVE")
    S.op("vector", lambda e: e.tensor_tensor(
        R2[:, :], V(R5, 0, [[16, 16], [1, 16]]),
        V(R4, 0, [[1, 16], [0, 16]]), ALU.is_ge), inc="sDVE")
    S.op("vector", lambda e: e.tensor_tensor(R3[:, :], scoreA[:, :], R2[:, :],
                                             ALU.mult), inc="sDVE")
    S.op("vector", lambda e: e.reduce_sum(
        V(R4, 16, [[1, 16]]), V(R3, 0, [[16, 16], [1, 16]]), AX.X), inc="sDVE")
    S.op("vector", lambda e: e.tensor_scalar(
        V(R4, 32, [[1, 16]]), V(R4, 16, [[1, 16]]), 1e-20, None, op0=ALU.add),
        inc="sDVE")
    S.op("vector", lambda e: e.reciprocal(
        V(R4, 32, [[1, 16]]), V(R4, 32, [[1, 16]])), inc="sDVE")
    S.op("vector", lambda e: e.tensor_tensor(
        R1[:, :], R3[:, :], V(R4, 32, [[1, 16], [0, 16]]), ALU.mult),
        inc="sDVE")
    dve_c = S.cnt["sDVE"]
    return dve_t, act_t, act_s, dve_c


def build_P4(S, gpool, seg, sgw, suw, sdw, wnat, wgT, wuT, wdT, ssg,
             lnf, x_res, ident, pb):
    lnf_ready = S.now("sDVE", "sACT")
    for cb in range(4):
        for wi, (wdr, dstT, iscol) in enumerate(
                ((sgw, wgT, False), (suw, wuT, False), (sdw, wdT, True))):
            for c4 in range(4):
                ch = cb * 4 + c4
                buf = wnat if (wi * 4 + c4) % 2 == 0 else ssg
                st = gpool.stage()
                if not iscol:
                    src = dap(wdr, ch * 128 * DM, [[DM, 128], [1, DM]])
                else:
                    src = dap(wdr, ch * 128,
                              [[HSH, 128], [128 * HSH, 8], [1, 128]])
                st.dma(buf[:, 0:1024], src, waits=S.now("sPE"))
                wwt = st.wait()
                for half in range(2):
                    for j in range(4):
                        jj = half * 4 + j
                        S.op("tensor", lambda e, buf=buf, jj=jj, j=j:
                             e.transpose(
                                 pb[7][:, j * 128:j * 128 + 128],
                                 buf[:, jj * 128:jj * 128 + 128], ident[:, :]),
                             waits=[wwt] + S.now("sDVE", "sACT"), inc="sPE")
                    pe_t = S.cnt["sPE"]
                    dst = dstT[:, c4 * 1024 + half * 512:
                               c4 * 1024 + half * 512 + 512]
                    if half == 0:
                        S.op("vector", lambda e, dst=dst: e.tensor_copy(
                            dst, pb[7][:, 0:512]),
                            waits=[("sPE", pe_t)], inc="sDVE")
                    else:
                        S.op("scalar", lambda e, dst=dst: e.copy(
                            dst, pb[7][:, 0:512]),
                            waits=[("sPE", pe_t)], inc="sACT")
        wT_ready = S.now("sDVE", "sACT")
        silu_act = [0, 0]
        multu_dve = 0
        for tg in range(8):
            pend = None
            for c4 in range(4):
                par = c4 % 2
                for kc in range(KC):
                    S.op("tensor", lambda e, c4=c4, kc=kc, tg=tg, par=par:
                         e.matmul(
                             pb[par][:, 0:256],
                             wgT[:, c4 * 1024 + kc * 128:
                                 c4 * 1024 + kc * 128 + 128],
                             lnf[:, kc * SEG + tg * 256:
                                 kc * SEG + tg * 256 + 256],
                             start=(kc == 0), stop=(kc == KC - 1)),
                         waits=lnf_ready + wT_ready +
                         [("sACT", silu_act[par])], inc="sPE")
                for kc in range(KC):
                    S.op("tensor", lambda e, c4=c4, kc=kc, tg=tg:
                         e.matmul(
                             pb[2][:, 0:256],
                             wuT[:, c4 * 1024 + kc * 128:
                                 c4 * 1024 + kc * 128 + 128],
                             lnf[:, kc * SEG + tg * 256:
                                 kc * SEG + tg * 256 + 256],
                             start=(kc == 0), stop=(kc == KC - 1)),
                         waits=[("sDVE", multu_dve)], inc="sPE")
                pe_gu = S.cnt["sPE"]
                if pend is not None:
                    c4p, dve_p = pend
                    for t2 in range(2):
                        for fs in range(2):
                            ob = 3 + t2 * 2 + fs
                            S.op("tensor", lambda e, c4p=c4p, t2=t2, fs=fs,
                                 ob=ob: e.matmul(
                                     pb[ob][:, 0:512],
                                     ssg[:, c4p * 256 + t2 * 128:
                                         c4p * 256 + t2 * 128 + 128],
                                     wdT[:, c4p * 1024 + fs * 512:
                                         c4p * 1024 + fs * 512 + 512],
                                     start=(c4p == 0), stop=False),
                                 waits=[("sDVE", dve_p)], inc="sPE")
                S.op("scalar", lambda e, c4=c4, par=par:
                     e.activation(ssg[:, c4 * 256:c4 * 256 + 256],
                                  pb[par][:, 0:256], AF.Silu),
                     waits=[("sPE", pe_gu)], inc="sACT")
                silu_act[par] = S.cnt["sACT"]
                S.op("vector", lambda e, c4=c4:
                     e.tensor_tensor(ssg[:, c4 * 256:c4 * 256 + 256],
                                     ssg[:, c4 * 256:c4 * 256 + 256],
                                     pb[2][:, 0:256], ALU.mult),
                     waits=[("sACT", silu_act[par])], inc="sDVE")
                multu_dve = S.cnt["sDVE"]
                pend = (c4, multu_dve)
            c4p, dve_p = pend
            for t2 in range(2):
                for fs in range(2):
                    ob = 3 + t2 * 2 + fs
                    S.op("tensor", lambda e, c4p=c4p, t2=t2, fs=fs, ob=ob:
                         e.matmul(pb[ob][:, 0:512],
                                  ssg[:, c4p * 256 + t2 * 128:
                                      c4p * 256 + t2 * 128 + 128],
                                  wdT[:, c4p * 1024 + fs * 512:
                                      c4p * 1024 + fs * 512 + 512],
                                  start=False, stop=True),
                         waits=[("sDVE", dve_p)], inc="sPE")
            pe_dn = S.cnt["sPE"]
            for t2 in range(2):
                for fs in range(2):
                    ob = 3 + t2 * 2 + fs
                    col = (tg * 2 + t2) * 1024 + fs * 512
                    S.op("vector", lambda e, ob=ob, col=col: e.tensor_tensor(
                        x_res[:, col:col + 512], x_res[:, col:col + 512],
                        pb[ob][:, 0:512], ALU.add),
                        waits=[("sPE", pe_dn)], inc="sDVE")


def build_P5(S, gpool, seg, gw, uw, dw, sgw, suw, sdw, weg, weu, wed, hw_sb,
             R1, lnf, lnf_td, act_s, dve_c, rt_guard, x_res, ident, pb):
    dve_t, act_t = lnf_td
    NSH = HSH // EH
    NEI = NSH + NE

    estage = {}
    gu_pe = {}
    dn_pe = {}

    def load_expert(ei, prev_gu, prev_dn, extra=()):
        par = ei % 2
        st = gpool.stage()
        if ei < NSH:
            sg = dap(sgw, ei * EH, [[HSH, 128], [128 * HSH, 8], [1, EH]])
            su = dap(suw, ei * EH, [[HSH, 128], [128 * HSH, 8], [1, EH]])
            sd = dap(sdw, ei * EH * DM, [[DM, 128], [128 * DM, 2], [1, DM]])
        else:
            e_ = ei - NSH
            sg = dap(gw, e_ * DM * EH, [[EH, 128], [128 * EH, 8], [1, EH]])
            su = dap(uw, e_ * DM * EH, [[EH, 128], [128 * EH, 8], [1, EH]])
            sd = dap(dw, e_ * EH * DM, [[DM, 128], [128 * DM, 2], [1, DM]])
        st.dma(weg[par][:, :], sg, waits=list(extra) + [("sPE", prev_gu)])
        st.dma(weu[par][:, :], su, waits=[("sPE", prev_gu)])
        st.dma(wed[par][:, :], sd, waits=[("sPE", prev_dn)])
        estage[ei] = st

    load_expert(0, 0, 0, extra=rt_guard)
    load_expert(1, 0, 0)

    units = [(e_, G) for e_ in range(NEI) for G in range(4)]
    silu_prev = 0
    mult_prev = 0
    hw_free = [0, 0]
    bank_free = [0, 0]
    pend = None
    g_done = [0] * NT

    def emit_down(p):
        uidx, par_u, par_e, e_, dve_m = p
        ee_, G_ = units[uidx]
        for t2 in range(4):
            bp = t2 % 2
            for fs in range(2):
                ob = 4 + bp * 2 + fs
                for hc in range(2):
                    S.op("tensor", lambda e, hc=hc, t2=t2, fs=fs, ob=ob,
                         par_u=par_u, par_e=par_e: e.matmul(
                             pb[ob][:, 0:512],
                             hw_sb[:, par_u * 1024 + hc * 512 + t2 * 128:
                                   par_u * 1024 + hc * 512 + t2 * 128 + 128],
                             wed[par_e][:, hc * 1024 + fs * 512:
                                        hc * 1024 + fs * 512 + 512],
                             start=(hc == 0), stop=(hc == 1)),
                         waits=[("sDVE", dve_m),
                                ("sDVE", bank_free[bp])], inc="sPE")
            pe_dn = S.cnt["sPE"]
            T = G_ * 4 + t2
            for fs in range(2):
                ob = 4 + bp * 2 + fs
                col = T * 1024 + fs * 512
                if e_ < NSH:
                    S.op("vector", lambda e, ob=ob, col=col: e.tensor_tensor(
                        x_res[:, col:col + 512], x_res[:, col:col + 512],
                        pb[ob][:, 0:512], ALU.add),
                        waits=[("sPE", pe_dn)], inc="sDVE")
                else:
                    er = e_ - NSH
                    S.op("vector", lambda e, ob=ob, col=col, T=T, er=er:
                         e.scalar_tensor_tensor(
                             x_res[:, col:col + 512], pb[ob][:, 0:512],
                             R1[:, T * 16 + er:T * 16 + er + 1],
                             x_res[:, col:col + 512],
                             op0=ALU.mult, op1=ALU.add),
                         waits=[("sPE", pe_dn), ("sDVE", dve_c)],
                         inc="sDVE")
            bank_free[bp] = S.cnt["sDVE"]
            if e_ == NEI - 1:
                g_done[G_ * 4 + t2] = S.cnt["sDVE"]
        dn_pe[e_] = S.cnt["sPE"]
        hw_free[par_u] = S.cnt["sPE"]

    for uidx, (e_, G) in enumerate(units):
        par_u = uidx % 2
        par_e = e_ % 2
        ww = estage[e_].wait()
        lnw = ([("sDVE", dve_t[G * 4 + 3]), ("sACT", act_t[G * 4 + 3])]
               if e_ == 0 else [])
        for hc in range(2):
            for kc in range(KC):
                S.op("tensor", lambda e, hc=hc, kc=kc, G=G, par_e=par_e:
                     e.matmul(
                         pb[hc][:, 0:512],
                         weg[par_e][:, kc * 256 + hc * 128:
                                    kc * 256 + hc * 128 + 128],
                         lnf[:, kc * SEG + G * 512:kc * SEG + G * 512 + 512],
                         start=(kc == 0), stop=(kc == KC - 1)),
                     waits=[ww] + lnw +
                     [("sACT", silu_prev)], inc="sPE")
        pe_g = S.cnt["sPE"]
        uw_ = [("sACT", act_s)] if e_ == 0 else []
        for hc in range(2):
            for kc in range(KC):
                S.op("tensor", lambda e, hc=hc, kc=kc, G=G, par_e=par_e:
                     e.matmul(
                         pb[2 + hc][:, 0:512],
                         weu[par_e][:, kc * 256 + hc * 128:
                                    kc * 256 + hc * 128 + 128],
                         lnf[:, kc * SEG + G * 512:kc * SEG + G * 512 + 512],
                         start=(kc == 0), stop=(kc == KC - 1)),
                     waits=uw_ + [("sDVE", mult_prev)], inc="sPE")
        pe_u = S.cnt["sPE"]
        gu_pe[e_] = pe_u
        if pend is not None:
            emit_down(pend)
            ep = pend[3]
            if pend[0] % 4 == 3 and ep + 2 < NEI:
                load_expert(ep + 2, gu_pe[ep], dn_pe[ep])
        for hc in range(2):
            S.op("scalar", lambda e, hc=hc, par_u=par_u: e.activation(
                hw_sb[:, par_u * 1024 + hc * 512:
                      par_u * 1024 + hc * 512 + 512],
                pb[hc][:, 0:512], AF.Silu),
                waits=[("sPE", pe_g), ("sPE", hw_free[par_u])], inc="sACT")
        silu_prev = S.cnt["sACT"]
        for hc in range(2):
            S.op("vector", lambda e, hc=hc, par_u=par_u: e.tensor_tensor(
                hw_sb[:, par_u * 1024 + hc * 512:
                      par_u * 1024 + hc * 512 + 512],
                hw_sb[:, par_u * 1024 + hc * 512:
                      par_u * 1024 + hc * 512 + 512],
                pb[2 + hc][:, 0:512], ALU.mult),
                waits=[("sACT", silu_prev), ("sPE", pe_u)], inc="sDVE")
        mult_prev = S.cnt["sDVE"]
        pend = (uidx, par_u, par_e, e_, mult_prev)
    emit_down(pend)
    return g_done



def build_program():
    nc = bass.Bass("TRN2", target_bir_lowering=False, detect_race_conditions=False)

    x_in = nc.dram_tensor("x", [SEQ, DM], F32, kind="ExternalInput").ap()
    wq = nc.dram_tensor("wq", [DM, DM], F32, kind="ExternalInput").ap()
    bq = nc.dram_tensor("bq", [DM], F32, kind="ExternalInput").ap()
    wk = nc.dram_tensor("wk", [DM, DM], F32, kind="ExternalInput").ap()
    bk = nc.dram_tensor("bk", [DM], F32, kind="ExternalInput").ap()
    wv = nc.dram_tensor("wv", [DM, DM], F32, kind="ExternalInput").ap()
    bv = nc.dram_tensor("bv", [DM], F32, kind="ExternalInput").ap()
    rw = nc.dram_tensor("rw", [NE, DM], F32, kind="ExternalInput").ap()
    cbias = nc.dram_tensor("cbias", [NE], F32, kind="ExternalInput").ap()
    gw = nc.dram_tensor("gw", [NE, DM, EH], F32, kind="ExternalInput").ap()
    uw = nc.dram_tensor("uw", [NE, DM, EH], F32, kind="ExternalInput").ap()
    dw = nc.dram_tensor("dw", [NE, EH, DM], F32, kind="ExternalInput").ap()
    sgw = nc.dram_tensor("sgw", [DM, HSH], F32, kind="ExternalInput").ap()
    suw = nc.dram_tensor("suw", [DM, HSH], F32, kind="ExternalInput").ap()
    sdw = nc.dram_tensor("sdw", [HSH, DM], F32, kind="ExternalInput").ap()
    c_ident = nc.dram_tensor("c_ident", [128, 128], F32, kind="ExternalInput").ap()
    c_mask0 = nc.dram_tensor("c_mask0", [128, 128], F32, kind="ExternalInput").ap()
    c_msks = nc.dram_tensor("c_msks", [128, 128], F32, kind="ExternalInput").ap()
    c_poolA = nc.dram_tensor("c_poolA", [128, 128], F32, kind="ExternalInput").ap()
    c_poolB = nc.dram_tensor("c_poolB", [128, 128], F32, kind="ExternalInput").ap()
    out_t = nc.dram_tensor("out", [SEQ, DM], F32, kind="ExternalOutput").ap()
    taps = {}
    if DEBUG_TAPS:
        for nm, shp in TAP_LIST or (
                ("tap_ln1", [128, 16384]), ("tap_xa", [128, 16384]),
                        ("tap_ln2", [128, 16384]), ("tap_score", [128, 256]),
                        ("tap_cfm", [16, 2048]), ("tap_xs", [128, 16384]),
                        ("tap_q", [128, 3072]), ("tap_v", [128, 3120]),
                        ("tap_yac", [65, 4096])):
            taps[nm] = nc.dram_tensor(nm, shp, F32, kind="ExternalOutput").ap()

    sems = ["sPE", "sDVE", "sACT", "sPOOL", "sOUT", "sTAP",
            "sA0", "sA1", "sA2", "sB0", "sB1", "sB2"]
    S = Sched(nc, sems)
    spool = DmaPool(S, "sync", ["sA0", "sA1", "sA2"])
    gpool = DmaPool(S, "gpsimd", ["sB0", "sB1", "sB2"])

    stack = contextlib.ExitStack()

    with stack:
        for s_ in sems:
            S.sems[s_] = stack.enter_context(nc.semaphore(s_))

        def sb(name, shape):
            return stack.enter_context(nc.sbuf_tensor(name, shape, F32))

        def psg(name):
            return stack.enter_context(nc.psum_tensor(name, [128, 512], F32))

        x_res = sb("x_res", [128, NT * 1024])
        lnf = sb("lnf", [128, KC * SEG])
        ident = sb("ident", [128, 128])
        mask0 = sb("mask0", [128, 128])
        msks = sb("msks", [128, 128])
        poolA = sb("poolA", [128, 128])
        poolB = sb("poolB", [128, 128])
        bq_c = sb("bq_c", [128, 8])
        bk_c = sb("bk_c", [128, 8])
        bv_c = sb("bv_c", [128, 8])
        rwT = sb("rwT", [128, 128])
        cb_bc = sb("cb_bc", [128, 16])
        cb_row = sb("cb_row", [1, 16])

        pb = [psg(f"pb{i}") for i in range(8)]

        with nc.sbuf_tensor("rnat", [16, 1024], F32) as rnat:
            st0 = spool.stage()
            st0.dma(ident[:, :], c_ident)
            st0.dma(mask0[:, :], c_mask0)
            st0.dma(msks[:, :], c_msks)
            st0.dma(poolA[:, :], c_poolA)
            st0.dma(poolB[:, :], c_poolB)
            st0.dma(rnat[0:16, :], rw)
            st0.dma(cb_row[0:1, :], dap(cbias, 0, [[16, 1], [1, 16]]))
            st1 = gpool.stage()
            st1.dma(bq_c[:, :], dap(bq, 0, [[1, 128], [128, 8]]), slow=True)
            st1.dma(bk_c[:, :], dap(bk, 0, [[1, 128], [128, 8]]), slow=True)
            st1.dma(bv_c[:, :], dap(bv, 0, [[1, 128], [128, 8]]), slow=True)
            w0 = st0.wait()
            w1 = st1.wait()

            for kc in range(KC):
                S.op("tensor", lambda e, kc=kc: e.transpose(
                    pb[5][:, kc * 16:kc * 16 + 16],
                    rnat[0:16, kc * 128:kc * 128 + 128], ident[0:16, 0:16]),
                    waits=[w0], inc="sPE")
            lhs_ones = sap(ident, 0, [[128, 1], [0, 128]])
            S.op("tensor", lambda e: e.matmul(
                pb[6][:, 0:16], lhs_ones, cb_row[0:1, :], start=True, stop=True),
                waits=[w0, w1], inc="sPE")
            tpre = S.cnt["sPE"]
            S.op("vector", lambda e: e.tensor_copy(rwT[:, :], pb[5][:, 0:128]),
                 waits=[("sPE", tpre)], inc="sDVE")
            S.op("vector", lambda e: e.tensor_copy(cb_bc[:, :], pb[6][:, 0:16]),
                 waits=[("sPE", tpre)], inc="sDVE")
            S.op("scalar", lambda e: e.activation(
                cb_row[0:1, 0:1], ident[0:1, 0:1], AF.Exp),
                waits=[("sPE", tpre)], inc="sACT")
            S.emit_block()

        for seg in range(NSEG):
            with contextlib.ExitStack() as att_scope:
                def sba(name, shape):
                    return att_scope.enter_context(
                        nc.sbuf_tensor(f"{name}_s{seg}", shape, F32))

                qT = sba("qT", [128, 3072])
                kT = sba("kT", [128, 3072])
                vT = sba("vT", [128, 3120])
                Yac = sba("Yac", [128, 4096])
                nm_sb = sba("nm_sb", [128, 32])
                wq_sl = sba("wq_sl", [128, 1024])
                wk_sl = sba("wk_sl", [128, 1024])
                wv_sl = sba("wv_sl", [128, 1024])

                xst = [spool.stage(), gpool.stage()]
                for hf in range(2):
                    src = dap(x_in, (seg * SEG + hf * 1024) * DM,
                              [[DM, 128], [128 * DM, 8], [1, DM]])
                    xst[hf].dma(x_res[:, hf * 8192:hf * 8192 + 8192], src,
                                waits=[("sOUT", 64 * seg)] + S.now("sDVE", "sPE"))
                xw = [xst[0].wait(), xst[1].wait()]
                for e in ENGINES:
                    S.fence_engine(e)
                with nc.sbuf_tensor(f"ln_tm1_s{seg}", [128, 1024], F32) as ln_tm, \
                     nc.sbuf_tensor(f"stat1_s{seg}", [128, 16], F32) as stat:
                    ln_tiles(S, x_res, lnf, ln_tm, stat, ident, pb,
                             lambda t: [xw[t // 8]])
                    S.emit_block()
                S_sb = sba("S_sb", [128, 1536])
                A_sb = sba("A_sb", [128, 1536])

                def tap(nm, src):
                    if DEBUG_TAPS and seg == 0 and nm in taps:
                        S.op("sync", lambda e, d=taps[nm], s=src:
                             e.dma_start(d, s),
                             waits=S.now("sDVE", "sACT", "sPE", "sPOOL"),
                             inc="sTAP", n=16)
                tap("tap_ln1", lnf[:, :])

                if not SKIP_P2:
                    build_P2(S, spool, seg, wq, wk, wv, wq_sl, wk_sl, wv_sl,
                             lnf, qT, kT, vT, Yac, S_sb, A_sb, nm_sb,
                             x_res, ident, mask0, msks, poolA, poolB,
                             bq_c, bk_c, bv_c, pb)
                tap("tap_xa", x_res[:, :])
                tap("tap_q", qT[:, :])
                tap("tap_v", vT[:, :])
                tap("tap_yac", Yac[0:65, :])
                S.emit_block()

            with contextlib.ExitStack() as ffn_scope:
                def sbf(name, shape, prt=128):
                    return ffn_scope.enter_context(
                        nc.sbuf_tensor(f"{name}_s{seg}", [prt, shape], F32))

                scoreA = sbf("scoreA", 256)
                R1 = sbf("R1", 256)
                R2 = sbf("R2", 256)
                R3 = sbf("R3", 256)
                R4 = sbf("R4", 256)
                R5 = sbf("R5", 256)
                ln_tm2 = sbf("ln_tm2", 1024)
                stat2 = sbf("stat2", 16)
                weg = [sbf(f"weg{i}", 2048) for i in range(2)]
                weu = [sbf(f"weu{i}", 2048) for i in range(2)]
                wed = [sbf(f"wed{i}", 2048) for i in range(2)]
                hw_sb = sbf("hw", 2048)

                for e in ENGINES:
                    S.fence_engine(e)
                rt_guard = S.now("sPE", "sDVE", "sACT")
                if not SKIP_P3:
                    p3r = build_P3(S, seg, x_res, lnf, ln_tm2, stat2, ident,
                                   rwT, cb_bc, scoreA, R1, R2, R3, R4, R5, pb)
                else:
                    p3r = ([0] * NT, [0] * NT, 0, 0)
                dve_t2, act_t2, act_s, dve_c = p3r
                tap("tap_ln2", lnf[:, :])
                tap("tap_score", scoreA[:, :])
                if not SKIP_P5:
                    g_done = build_P5(S, gpool, seg, gw, uw, dw, sgw, suw,
                                      sdw, weg, weu, wed, hw_sb, R1, lnf,
                                      (dve_t2, act_t2), act_s, dve_c,
                                      rt_guard, x_res, ident, pb)
                else:
                    g_done = [S.cnt["sDVE"]] * NT
                for G in range(4):
                    dst = dap(out_t, (seg * SEG + G * 512) * DM,
                              [[DM, 128], [128 * DM, 4], [1, DM]])
                    S.op("sync", lambda e, dst=dst, G=G: e.dma_start(
                        dst, x_res[:, G * 4096:G * 4096 + 4096]),
                        waits=[("sDVE", g_done[G * 4 + 3])],
                        inc="sOUT", n=16)
                S.emit_block()

        S.op("sync", None, waits=[("sOUT", S.cnt["sOUT"]),
                                  ("sTAP", S.cnt["sTAP"])])
        S.emit_block()

    return nc


def make_consts():
    ident = np.eye(128, dtype=np.float32)
    bi = np.arange(128) // 16
    mask0 = np.where(bi[:, None] == bi[None, :], 0.0, MASKNEG).astype(np.float32)
    msks = np.where((bi[:, None] ^ 1) == bi[None, :], 0.0, MASKNEG).astype(np.float32)
    poolA = np.zeros((128, 128), dtype=np.float32)
    poolB = np.zeros((128, 128), dtype=np.float32)
    for m in range(64):
        poolA[2 * m, m] = 1.0
        poolA[2 * m + 1, m] = 1.0
        poolB[2 * m, 64 + m] = 1.0
        poolB[2 * m + 1, 64 + m] = 1.0
    return ident, mask0, msks, poolA, poolB


def kernel(L0_attn_hidden_states, L0_attn_Wq, L0_attn_bq, L0_attn_Wk,
           L0_attn_bk, L0_attn_Wv, L0_attn_bv, L0_ffn_router_w,
           L0_ffn_correction_bias, L0_ffn_gate_w, L0_ffn_up_w, L0_ffn_down_w,
           L0_ffn_shared_gate_w, L0_ffn_shared_up_w, L0_ffn_shared_down_w):
    x = np.ascontiguousarray(np.asarray(L0_attn_hidden_states, dtype=np.float32))
    ident, mask0, msks, poolA, poolB = make_consts()

    def c32(a):
        return np.ascontiguousarray(np.asarray(a, dtype=np.float32))

    common = {
        "wq": c32(L0_attn_Wq), "bq": c32(L0_attn_bq),
        "wk": c32(L0_attn_Wk), "bk": c32(L0_attn_bk),
        "wv": c32(L0_attn_Wv), "bv": c32(L0_attn_bv),
        "rw": c32(L0_ffn_router_w), "cbias": c32(L0_ffn_correction_bias),
        "gw": c32(L0_ffn_gate_w), "uw": c32(L0_ffn_up_w),
        "dw": c32(L0_ffn_down_w),
        "sgw": c32(np.asarray(L0_ffn_shared_gate_w).T),
        "suw": c32(np.asarray(L0_ffn_shared_up_w).T),
        "sdw": c32(np.asarray(L0_ffn_shared_down_w).T),
        "c_ident": ident, "c_mask0": mask0, "c_msks": msks,
        "c_poolA": poolA, "c_poolB": poolB,
    }
    nc = build_program()
    in_maps = [dict(common, x=np.ascontiguousarray(x[c])) for c in range(NCORE)]
    res = run_bass_kernel_spmd(nc, in_maps, core_ids=list(range(NCORE)))
    out = np.stack([np.asarray(r["out"], dtype=np.float32)
                    for r in res.results], axis=0)
    return out


# version 20
# speedup vs baseline: 1.1282x; 1.1282x over previous
import contextlib
import numpy as np
import concourse.bass as bass
import concourse.mybir as mybir
from concourse.bass_utils import run_bass_kernel_spmd

F32 = mybir.dt.float32
AF = mybir.ActivationFunctionType
ALU = mybir.AluOpType
AX = mybir.AxisListType

NCORE = 8
DM = 1024
SEQ = 4096
SEG = 2048
NSEG = 2
NT = 16
KC = 8
NPAIR = 8
DH = 64
NLEV = 7
LVL_LEN = [2048, 1024, 512, 256, 128, 64, 32]
NE = 16
EH = 256
HSH = 2048
BIG = 1.0e30
MASKNEG = -1.0e9
DEBUG_TAPS = False
TAP_LIST = None
EARLY_POOL = True
PERTILE = 2
SKIP_P2 = False
SKIP_P3 = False
SKIP_P4 = False
SKIP_P5 = False

ENGINES = ("tensor", "vector", "scalar", "gpsimd", "sync")


def dap(apobj, offset, dims):
    return bass.AP(apobj.tensor, offset, dims)


def sap(handle, offset, dims):
    return bass.AP(handle, offset, dims)


class Sched:
    def __init__(self, nc, semnames):
        self.nc = nc
        self.semnames = list(semnames)
        self.sems = {}
        self.cnt = {k: 0 for k in semnames}
        self.ops = {e: [] for e in ENGINES}
        self.lastw = {e: {k: 0 for k in semnames} for e in ENGINES}

    def op(self, eng, fn, waits=(), inc=None, n=1):
        if eng == "vector":
            waits = list(waits) + [("sDVE", self.cnt["sDVE"])]
            if inc is None and fn is not None:
                inc = "sDVE"
        wl = []
        for s, v in waits:
            if v > 0 and v > self.lastw[eng][s]:
                wl.append((s, v))
                self.lastw[eng][s] = v
        self.ops[eng].append((fn, wl, inc, n))
        if inc is not None:
            self.cnt[inc] += n
            return self.cnt[inc]
        return None

    def now(self, *names):
        return [(s, self.cnt[s]) for s in names]

    def fence_engine(self, eng):
        for s in self.semnames:
            v = self.cnt[s]
            if v > self.lastw[eng][s]:
                self.ops[eng].append((None, [(s, v)], None, 0))
                self.lastw[eng][s] = v

    def emit_block(self):
        ops = self.ops
        self.ops = {e: [] for e in ENGINES}
        sems = self.sems
        with self.nc.Block() as blk:
            for eng in ENGINES:
                if not ops[eng]:
                    continue

                def body(ee, _lst=ops[eng]):
                    for fn, wl, inc, n in _lst:
                        for s, v in wl:
                            ee.wait_ge(sems[s], v)
                        if fn is None:
                            continue
                        ins = fn(ee)
                        if inc is not None:
                            ins.then_inc(sems[inc], n)

                getattr(blk, eng)(body)


class DmaStage:
    def __init__(self, sched, eng, sem):
        self.s = sched
        self.eng = eng
        self.sem = sem

    def dma(self, out, in_, waits=(), slow=False):
        if slow:
            fn = lambda e, o=out, i=in_: e.dma_start(
                o, i, allow_slow_non_contiguous=True)
        else:
            fn = lambda e, o=out, i=in_: e.dma_start(o, i)
        self.s.op(self.eng, fn, waits=waits, inc=self.sem, n=16)

    def wait(self):
        return (self.sem, self.s.cnt[self.sem])


class DmaPool:
    def __init__(self, sched, eng, semnames):
        self.s = sched
        self.eng = eng
        self.names = list(semnames)
        self.i = 0

    def stage(self):
        sem = self.names[self.i % len(self.names)]
        self.i += 1
        return DmaStage(self.s, self.eng, sem)


def ln_tiles(S, x_res, lnf, ln_tm, stat, ident, pb, tile_extra_waits,
             after_tile=None):
    hist = {"pe_tr": [0, 0], "act_sq": [0, 0]}
    dve_t = []
    act_t = []
    for t in range(NT):
        par = 0
        so = 0
        lo = 0
        xt = x_res[:, t * 1024:(t + 1) * 1024]
        war = [("sPE", hist["pe_tr"][par]), ("sACT", hist["act_sq"][par])]
        S.op("vector", lambda e, xt=xt, so=so: e.reduce_sum(
            stat[:, so:so + 1], xt, AX.X),
            waits=tile_extra_waits(t) + war, inc="sDVE")
        S.op("vector", lambda e, so=so: e.tensor_scalar(
            stat[:, so + 1:so + 2], stat[:, so:so + 1], -1.0 / DM, None,
            op0=ALU.mult), inc="sDVE")
        dve_m = S.cnt["sDVE"]
        S.op("scalar", lambda e, xt=xt, so=so, lo=lo: e.activation(
            ln_tm[:, lo:lo + 1024], xt, AF.Square, bias=stat[:, so + 1:so + 2],
            accum_out=stat[:, so + 2:so + 3]),
            waits=[("sDVE", dve_m), ("sPE", hist["pe_tr"][par])], inc="sACT")
        act_sq = S.cnt["sACT"]
        hist["act_sq"][par] = act_sq
        S.op("vector", lambda e, so=so: e.tensor_scalar(
            stat[:, so + 3:so + 4], stat[:, so + 2:so + 3], 1.0 / DM, 1e-5,
            op0=ALU.mult, op1=ALU.add),
            waits=[("sACT", act_sq)], inc="sDVE")
        dve_v = S.cnt["sDVE"]
        S.op("scalar", lambda e, so=so: e.activation(
            stat[:, so + 4:so + 5], stat[:, so + 3:so + 4], AF.Sqrt),
            waits=[("sDVE", dve_v)], inc="sACT")
        act_rt = S.cnt["sACT"]
        S.op("vector", lambda e, so=so: e.reciprocal(
            stat[:, so + 5:so + 6], stat[:, so + 4:so + 5]),
            waits=[("sACT", act_rt)], inc="sDVE")
        S.op("vector", lambda e, xt=xt, so=so, lo=lo: e.tensor_scalar(
            ln_tm[:, lo:lo + 1024], xt, stat[:, so + 1:so + 2],
            stat[:, so + 5:so + 6], op0=ALU.add, op1=ALU.mult),
            waits=[("sACT", act_sq)], inc="sDVE")
        dve_n = S.cnt["sDVE"]
        for half in range(2):
            for j in range(4):
                kc = half * 4 + j
                S.op("tensor", lambda e, kc=kc, j=j, lo=lo: e.transpose(
                    pb[6][:, j * 128:j * 128 + 128],
                    ln_tm[:, lo + kc * 128:lo + kc * 128 + 128], ident[:, :]),
                    waits=[("sDVE", max(dve_n, S.cnt["sDVE"])),
                           ("sACT", S.cnt["sACT"])],
                    inc="sPE")
            pe_tr = S.cnt["sPE"]
            dst = sap(lnf, (half * 4) * SEG + t * 128,
                      [[int(lnf.shape[1]), 128], [SEG, 4], [1, 128]])
            if half == 0:
                S.op("vector", lambda e, dst=dst: e.tensor_copy(
                    dst, pb[6][:, 0:512]), waits=[("sPE", pe_tr)], inc="sDVE")
            else:
                S.op("scalar", lambda e, dst=dst: e.copy(
                    dst, pb[6][:, 0:512]), waits=[("sPE", pe_tr)], inc="sACT")
        hist["pe_tr"][par] = S.cnt["sPE"]
        dve_t.append(S.cnt["sDVE"])
        act_t.append(S.cnt["sACT"])
        if after_tile is not None:
            after_tile(t)
    return dve_t, act_t


def build_P2(S, spool, seg, wq, wk, wv, wq_sl, wk_sl, wv_sl, lnf,
             qT, kT, vT, Yac, S_sb, A_sb, nm_sb, x_res,
             ident, mask0, msks, poolA, poolB, bq_c, bk_c, bv_c, pb):
    nvT = int(vT.shape[1])

    S.op("gpsimd", lambda e: e.memset(
        sap(vT, 64, [[nvT, 128], [130, 24], [65, 2]]), 1.0), inc="sPOOL")
    ones_done = S.cnt["sPOOL"]

    wstage = [None] * (NPAIR + 1)
    wstage[0] = spool.stage()
    for wsl, wdr in ((wq_sl, wq), (wk_sl, wk), (wv_sl, wv)):
        wstage[0].dma(wsl[:, :],
                      dap(wdr, 0, [[DM, 128], [128 * DM, 8], [1, 128]]))

    lnf_ready = S.now("sDVE", "sACT")
    evac_act = {}
    pair_dve = 0
    pair_act = 0

    for p in range(NPAIR):
        ww = wstage[p].wait()
        for g in range(4):
            gp = g % 2
            for (wsl, bcol, dstbuf, bank) in (
                    (wq_sl, bq_c, qT, 0 + 2 * gp), (wk_sl, bk_c, kT, 1 + 2 * gp)):
                for kc in range(KC):
                    S.op("tensor", lambda e, wsl=wsl, kc=kc, g=g, bank=bank:
                         e.matmul(
                             pb[bank][:, 0:512],
                             wsl[:, kc * 128:kc * 128 + 128],
                             lnf[:, kc * SEG + g * 512:kc * SEG + g * 512 + 512],
                             start=(kc == 0), stop=(kc == KC - 1)),
                         waits=[ww, ("sDVE", pair_dve)] + lnf_ready +
                         [("sACT", max(evac_act.get((bank,), 0), pair_act))],
                         inc="sPE")
                pe_mm = S.cnt["sPE"]
                S.op("scalar", lambda e, dstbuf=dstbuf, g=g, bank=bank,
                     bcol=bcol, p=p: e.activation(
                         dstbuf[:, g * 512:g * 512 + 512], pb[bank][:, 0:512],
                         AF.Identity, bias=bcol[:, p:p + 1]),
                     waits=[("sPE", pe_mm)], inc="sACT")
                evac_act[(bank,)] = S.cnt["sACT"]
            vbank = 4 + gp
            for kc in range(KC):
                S.op("tensor", lambda e, kc=kc, g=g, vbank=vbank: e.matmul(
                    pb[vbank][:, 0:512], wv_sl[:, kc * 128:kc * 128 + 128],
                    lnf[:, kc * SEG + g * 512:kc * SEG + g * 512 + 512],
                    start=(kc == 0), stop=(kc == KC - 1)),
                    waits=[ww, ("sDVE", pair_dve)] +
                    [("sACT", max(evac_act.get((vbank,), 0), pair_act))],
                    inc="sPE")
            pe_v = S.cnt["sPE"]
            S.op("scalar", lambda e, p=p, gp=gp, vbank=vbank: e.activation(
                S_sb[:, gp * 512:gp * 512 + 512], pb[vbank][:, 0:512],
                AF.Identity, bias=bv_c[:, p:p + 1]),
                waits=[("sPE", pe_v)] + S.now("sDVE"),
                inc="sACT")
            act_v = S.cnt["sACT"]
            evac_act[(vbank,)] = act_v
            tb_ = 6 + gp
            for i in range(4):
                S.op("tensor", lambda e, i=i, gp=gp, tb_=tb_: e.transpose(
                    pb[tb_][:, i * 128:i * 128 + 128],
                    S_sb[:, gp * 512 + i * 128:gp * 512 + i * 128 + 128],
                    ident[:, :]),
                    waits=[("sACT", act_v),
                           ("sDVE", evac_act.get((6, gp), 0))], inc="sPE")
            pe_tr = S.cnt["sPE"]
            dst = sap(vT, g * 4 * 130, [[nvT, 128], [130, 4], [65, 2], [1, 64]])
            srcp = sap(pb[tb_], 0, [[512, 128], [128, 4], [64, 2], [1, 64]])
            S.op("vector", lambda e, dst=dst, srcp=srcp: e.tensor_copy(dst, srcp),
                 waits=[("sPE", pe_tr), ("sPOOL", ones_done)], inc="sDVE")
            evac_act[(6, gp)] = S.cnt["sDVE"]
            evac_act[(4, gp)] = S.cnt["sDVE"]
        qkv_pe = S.cnt["sPE"]
        qkv_act = S.cnt["sACT"]

        if p + 1 < NPAIR:
            wstage[p + 1] = spool.stage()
            for wsl, wdr in ((wq_sl, wq), (wk_sl, wk), (wv_sl, wv)):
                wstage[p + 1].dma(
                    wsl[:, :], dap(wdr, (p + 1) * 128,
                                   [[DM, 128], [128 * DM, 8], [1, 128]]),
                    waits=[("sPE", qkv_pe)])

        attend_hist = []
        prev = {"act_e": [0, 0, 0], "pe_y": [0, 0, 0],
                "act_e_i": [None, None, None],
                "dve_sst": [0, 0, 0], "dve_ya": [0, 0]}
        bglob = [0]
        pend = [None]

        def emit_S(lv, h, t0, nt_b, w, qoff, bp, ypar):
            sbank = (2, 3, 0)[bp]
            scale = 0.125 * (0.25 ** lv)
            maska = mask0 if lv == 0 else msks
            so_ = bp * 512
            pdims = [[512, w], [128, nt_b], [1, w]]
            sdims = [[1536, w], [128, nt_b], [1, w]]
            pe_s_i = []
            for i in range(nt_b):
                co = qoff + (t0 + i) * 128
                S.op("tensor", lambda e, h=h, co=co, w=w, i=i, sbank=sbank:
                     e.matmul(
                         pb[sbank][0:w, i * 128:i * 128 + w],
                         qT[h * 64:h * 64 + 64, co:co + w],
                         kT[h * 64:h * 64 + 64, co:co + w],
                         start=True, stop=True),
                     waits=[("sDVE", prev["dve_sst"][bp]),
                            ("sACT", qkv_act)],
                     inc="sPE")
                pe_s_i.append(S.cnt["sPE"])
            S.op("vector", lambda e, sbank=sbank, pdims=pdims, sdims=sdims,
                 w=w, nt_b=nt_b, so_=so_, scale=scale, maska=maska:
                 e.scalar_tensor_tensor(
                     sap(S_sb, so_, sdims), sap(pb[sbank], 0, pdims),
                     scale, sap(maska, 0, [[128, w], [0, nt_b], [1, w]]),
                     op0=ALU.mult, op1=ALU.add),
                 waits=[("sPE", pe_s_i[-1]),
                        ("sACT", prev["act_e"][bp])], inc="sDVE")
            prev["dve_sst"][bp] = S.cnt["sDVE"]
            S.op("vector", lambda e, sdims=sdims, w=w, nt_b=nt_b,
                 bp=bp, so_=so_: e.reduce_max(
                     nm_sb[0:w, bp * 4:bp * 4 + nt_b],
                     sap(S_sb, so_, sdims), AX.X, negate=True),
                 inc="sDVE")
            dve_mx = S.cnt["sDVE"]
            acc = []
            for i in range(nt_b):
                S.op("scalar", lambda e, i=i, w=w, so_=so_, bp=bp:
                     e.activation(
                         sap(A_sb, so_ + i * 128, [[1536, w], [1, w]]),
                         sap(S_sb, so_ + i * 128, [[1536, w], [1, w]]),
                         AF.Exp,
                         bias=nm_sb[0:w, bp * 4 + i:bp * 4 + i + 1]),
                     waits=[("sDVE", dve_mx), ("sPE", prev["pe_y"][bp])],
                     inc="sACT")
                acc.append(S.cnt["sACT"])
            prev["act_e"][bp] = S.cnt["sACT"]
            prev["act_e_i"][bp] = acc

        def emit_rest(lv, h, t0, nt_b, w, slot0, bp, ypar):
            so_ = bp * 512
            tbank = (6, 7, 1)[bp]
            ybank = 4 + ypar
            act_i = prev["act_e_i"][bp]
            pe_t_i = []
            for i in range(nt_b):
                S.op("tensor", lambda e, i=i, w=w, tbank=tbank, so_=so_:
                     e.transpose(
                         pb[tbank][0:w, i * 128:i * 128 + w],
                         sap(A_sb, so_ + i * 128, [[1536, w], [1, w]]),
                         ident[0:w, 0:w]),
                     waits=[("sACT", act_i[i])], inc="sPE")
                pe_t_i.append(S.cnt["sPE"])
            S.op("scalar", lambda e, w=w, nt_b=nt_b, tbank=tbank,
                 so_=so_: e.copy(
                     sap(A_sb, so_, [[1536, w], [128, nt_b], [1, w]]),
                     sap(pb[tbank], 0, [[512, w], [128, nt_b], [1, w]])),
                 waits=[("sPE", pe_t_i[-1])], inc="sACT")
            cp_act = S.cnt["sACT"]
            for i in range(nt_b):
                slot = slot0 + t0 + i
                lhsT = sap(vT, slot * 130 + h * 65, [[nvT, w], [1, 65]])
                S.op("tensor", lambda e, lhsT=lhsT, i=i, w=w, so_=so_,
                     ybank=ybank: e.matmul(
                         pb[ybank][0:65, i * 128:i * 128 + w],
                         lhsT,
                         sap(A_sb, so_ + i * 128, [[1536, w], [1, w]]),
                         start=True, stop=True),
                     waits=[("sACT", cp_act),
                            ("sDVE", prev["dve_ya"][ypar]),
                            ("sPOOL", ones_done)], inc="sPE")
            pe_y = S.cnt["sPE"]
            prev["pe_y"][bp] = pe_y
            rep = 1 << lv
            for i in range(nt_b):
                span0 = h * 2048 + (t0 + i) * 128 * rep
                ydst = sap(Yac, span0, [[int(Yac.shape[1]), 65], [1, w * rep]])
                ysrc = sap(pb[ybank], i * 128, [[512, 65], [1, w], [0, rep]])
                if lv == 0:
                    S.op("vector", lambda e, ydst=ydst, ysrc=ysrc:
                         e.tensor_copy(ydst, ysrc),
                         waits=[("sPE", pe_y)], inc="sDVE")
                else:
                    S.op("vector", lambda e, ydst=ydst, ysrc=ysrc:
                         e.tensor_tensor(ydst, ydst, ysrc, ALU.add),
                         waits=[("sPE", pe_y)], inc="sDVE")
            prev["dve_ya"][ypar] = S.cnt["sDVE"]

        def flush():
            if pend[0] is not None:
                emit_rest(*pend[0])
                pend[0] = None

        qkv_vdve = S.cnt["sDVE"]
        qkadd_dve = [0] * (NLEV + 1)
        vpool_dve = [qkv_vdve] * NLEV

        for lv in range(NLEV):
            L = LVL_LEN[lv]
            ntl = (L + 127) // 128
            qoff = 0 if lv % 2 == 0 else 2048
            slot0 = 0 if lv % 2 == 0 else 16
            if lv > 0:
                flush()
                pslot0 = 0 if (lv - 1) % 2 == 0 else 16
                Lp = LVL_LEN[lv - 1]
                guard = ([("sPE", attend_hist[lv - 2])] if lv >= 2 else
                         [("sPE", qkv_pe)])
                if not EARLY_POOL:
                    psrc = 2048 - qoff
                    for buf in (qT, kT):
                        S.op("vector", lambda e, buf=buf, qoff=qoff,
                             psrc=psrc, L=L: e.tensor_tensor(
                                 buf[:, qoff:qoff + L],
                                 buf[:, psrc:psrc + 2 * L:2],
                                 buf[:, psrc + 1:psrc + 2 * L:2], ALU.add),
                             waits=guard, inc="sDVE")
                    qkadd_dve[lv] = S.cnt["sDVE"]
                n_src = (Lp + 127) // 128
                pool_cp = 0
                for j in range(ntl):
                    w_out = min(128, L - j * 128)
                    mats = []
                    for (pm, st_) in ((poolA, 2 * j), (poolB, 2 * j + 1)):
                        if st_ >= n_src:
                            continue
                        wi = min(128, Lp - st_ * 128)
                        if wi <= 0:
                            continue
                        mats.append((pm, st_, wi))
                    for mi, (pm, st_, wi) in enumerate(mats):
                        rhs = sap(vT, (pslot0 + st_) * 130,
                                  [[nvT, wi], [65, 2], [1, 64]])
                        if w_out < 128:
                            lhsT = pm[0:wi, 0:w_out]
                        else:
                            lhsT = pm[0:wi, :]
                        vw = (guard + [("sDVE", max(vpool_dve[lv - 1],
                                                    pool_cp,
                                                    prev["dve_sst"][2]))]
                              if EARLY_POOL else
                              guard + S.now("sACT", "sDVE"))
                        S.op("tensor", lambda e, lhsT=lhsT, rhs=rhs, mi=mi,
                             last=(mi == len(mats) - 1), w_out=w_out: e.matmul(
                                 pb[0][0:w_out, 0:128], lhsT, rhs,
                                 start=(mi == 0), stop=last),
                             waits=vw, inc="sPE")
                    pe_p = S.cnt["sPE"]
                    dstv = sap(vT, (slot0 + j) * 130,
                               [[nvT, w_out], [65, 2], [1, 64]])
                    srcv = sap(pb[0], 0, [[512, w_out], [64, 2], [1, 64]])
                    S.op("vector", lambda e, dstv=dstv, srcv=srcv:
                         e.tensor_copy(dstv, srcv),
                         waits=[("sPE", pe_p)], inc="sDVE")
                    pool_cp = S.cnt["sDVE"]
                vpool_dve[lv] = S.cnt["sDVE"]
                for ph in range(3):
                    prev["dve_sst"][ph] = max(prev["dve_sst"][ph],
                                              qkadd_dve[lv])
                prev["dve_sst"][2] = max(prev["dve_sst"][2], pool_cp)
            nbatch = (ntl + 3) // 4
            first = True
            for h in range(2):
                for b in range(nbatch):
                    t0 = b * 4
                    nt_b = min(4, ntl - t0)
                    w = min(128, L)
                    bp = bglob[0] % 3
                    ypar = bglob[0] % 2
                    bglob[0] += 1
                    emit_S(lv, h, t0, nt_b, w, qoff, bp, ypar)
                    this = (lv, h, t0, nt_b, w, slot0, bp, ypar)
                    if pend[0] is not None:
                        emit_rest(*pend[0])
                    pend[0] = this
                    if first and EARLY_POOL and lv + 1 < NLEV:
                        first = False
                        Ln = LVL_LEN[lv + 1]
                        qoff_n = 2048 - qoff
                        gpe = (attend_hist[lv - 1] if lv >= 1 else qkv_pe)
                        for buf in (qT, kT):
                            S.op("vector", lambda e, buf=buf, qoff_n=qoff_n,
                                 qoff=qoff, Ln=Ln: e.tensor_tensor(
                                     buf[:, qoff_n:qoff_n + Ln],
                                     buf[:, qoff:qoff + 2 * Ln:2],
                                     buf[:, qoff + 1:qoff + 2 * Ln:2],
                                     ALU.add),
                                 waits=[("sPE", gpe), ("sACT", qkv_act)],
                                 inc="sDVE")
                        qkadd_dve[lv + 1] = S.cnt["sDVE"]
            attend_hist.append(S.cnt["sPE"])
        flush()

        yacc_dve = S.cnt["sDVE"]
        for h in range(2):
            for tb in range(4):
                for i in range(4):
                    t = tb * 4 + i
                    S.op("tensor", lambda e, h=h, t=t, i=i: e.transpose(
                        pb[1][:, i * 128:i * 128 + 128],
                        Yac[:, h * 2048 + t * 128:h * 2048 + t * 128 + 128],
                        ident[:, :]),
                        waits=[("sDVE", max(yacc_dve, S.cnt["sDVE"])),
                               ("sACT", S.cnt["sACT"])], inc="sPE")
                pe_f = S.cnt["sPE"]
                rcol = 16 + (tb % 2) * 4
                S.op("vector", lambda e, rcol=rcol: e.tensor_scalar(
                    nm_sb[:, rcol:rcol + 4],
                    sap(pb[1], 64, [[512, 128], [128, 4]]),
                    1e-8, None, op0=ALU.add),
                    waits=[("sPE", pe_f)], inc="sDVE")
                S.op("vector", lambda e, rcol=rcol: e.reciprocal(
                    nm_sb[:, rcol:rcol + 4], nm_sb[:, rcol:rcol + 4]),
                    inc="sDVE")
                for i in range(4):
                    t = tb * 4 + i
                    pcol = t * 1024 + (p * 2 + h) * 64
                    S.op("vector", lambda e, i=i, pcol=pcol, rcol=rcol:
                         e.scalar_tensor_tensor(
                             x_res[:, pcol:pcol + 64],
                             pb[1][:, i * 128:i * 128 + 64],
                             nm_sb[:, rcol + i:rcol + i + 1],
                             x_res[:, pcol:pcol + 64],
                             op0=ALU.mult, op1=ALU.add), inc="sDVE")
        pair_dve = S.cnt["sDVE"]
        pair_act = S.cnt["sACT"]



def build_P3(S, seg, x_res, lnf, ln_tm, stat, ident, rwT, cb_bc,
             scoreA, R1, R2, R3, R4, R5, pb):
    def after_tile(t):
        lnw = S.now("sDVE", "sACT")
        for kc in range(KC):
            S.op("tensor", lambda e, t=t, kc=kc: e.matmul(
                pb[2][:, t * 16:t * 16 + 16],
                lnf[:, kc * SEG + t * 128:kc * SEG + t * 128 + 128],
                rwT[:, kc * 16:kc * 16 + 16],
                start=(kc == 0), stop=(kc == KC - 1)),
                waits=lnw, inc="sPE")

    dve_t, act_t = ln_tiles(S, x_res, lnf, ln_tm, stat, ident, pb,
                            lambda t: [], after_tile)
    lnf_ready = S.now("sDVE", "sACT")
    pe_r = S.cnt["sPE"]
    S.op("scalar", lambda e: e.activation(
        scoreA[:, :], pb[2][:, 0:256], AF.Sigmoid),
        waits=[("sPE", pe_r)], inc="sACT")
    act_s = S.cnt["sACT"]

    def V(ten, off, dims):
        return sap(ten, off, [[int(ten.shape[1]), 128]] + dims)

    dve0 = S.now("sDVE")
    S.op("vector", lambda e: e.tensor_tensor(
        R1[:, :], V(scoreA, 0, [[16, 16], [1, 16]]),
        V(cb_bc, 0, [[0, 16], [1, 16]]), ALU.add),
        waits=[("sACT", act_s)] + dve0, inc="sDVE")
    S.op("vector", lambda e: e.tensor_tensor(
        V(R2, 0, [[8, 16], [2, 4], [1, 2]]),
        V(R1, 0, [[16, 16], [4, 4], [2, 2]]),
        V(R1, 1, [[16, 16], [4, 4], [2, 2]]), ALU.max), inc="sDVE")
    S.op("vector", lambda e: e.tensor_tensor(
        V(R2, 128, [[8, 16], [2, 4], [1, 2]]),
        V(R1, 0, [[16, 16], [4, 4], [2, 2]]),
        V(R1, 1, [[16, 16], [4, 4], [2, 2]]), ALU.min), inc="sDVE")
    S.op("vector", lambda e: e.tensor_tensor(
        V(R3, 0, [[4, 16], [1, 4]]),
        V(R2, 0, [[8, 16], [2, 4]]), V(R2, 1, [[8, 16], [2, 4]]), ALU.max),
        inc="sDVE")
    S.op("vector", lambda e: e.tensor_tensor(
        V(R3, 64, [[4, 16], [1, 4]]),
        V(R2, 0, [[8, 16], [2, 4]]), V(R2, 1, [[8, 16], [2, 4]]), ALU.min),
        inc="sDVE")
    S.op("vector", lambda e: e.tensor_tensor(
        V(R3, 128, [[4, 16], [1, 4]]),
        V(R2, 128, [[8, 16], [2, 4]]), V(R2, 129, [[8, 16], [2, 4]]), ALU.max),
        inc="sDVE")
    S.op("vector", lambda e: e.tensor_tensor(
        V(R3, 192, [[4, 16], [1, 4]]),
        V(R3, 64, [[4, 16], [1, 4]]), V(R3, 128, [[4, 16], [1, 4]]), ALU.max),
        inc="sDVE")
    S.op("vector", lambda e: e.tensor_tensor(
        V(R4, 0, [[4, 16], [1, 4]]),
        V(R3, 0, [[4, 16], [1, 4]]), V(R3, 192, [[4, 16], [1, 4]]), ALU.add),
        inc="sDVE")
    S.op("vector", lambda e: e.tensor_tensor(
        V(R4, 64, [[2, 16], [1, 2]]),
        V(R4, 0, [[4, 16], [2, 2]]), V(R4, 1, [[4, 16], [2, 2]]), ALU.max),
        inc="sDVE")
    S.op("vector", lambda e: e.tensor_tensor(
        V(R4, 96, [[2, 16], [1, 2]]),
        V(R4, 0, [[4, 16], [2, 2]]), V(R4, 1, [[4, 16], [2, 2]]), ALU.min),
        inc="sDVE")
    S.op("vector", lambda e: e.tensor_tensor(
        V(R4, 128, [[1, 16]]),
        V(R4, 64, [[2, 16]]), V(R4, 65, [[2, 16]]), ALU.min), inc="sDVE")
    S.op("vector", lambda e: e.tensor_tensor(
        V(R4, 144, [[1, 16]]),
        V(R4, 96, [[2, 16]]), V(R4, 97, [[2, 16]]), ALU.max), inc="sDVE")
    S.op("vector", lambda e: e.tensor_tensor(
        V(R4, 160, [[1, 16]]),
        V(R4, 128, [[1, 16]]), V(R4, 144, [[1, 16]]), ALU.max), inc="sDVE")
    S.op("vector", lambda e: e.tensor_tensor(
        V(R4, 192, [[4, 16], [1, 4]]),
        V(R4, 0, [[4, 16], [1, 4]]),
        V(R4, 160, [[1, 16], [0, 4]]), ALU.is_ge), inc="sDVE")
    S.op("vector", lambda e: e.tensor_scalar(
        R2[:, :], V(R4, 192, [[4, 16], [1, 4], [0, 4]]),
        1.0, BIG, op0=ALU.subtract, op1=ALU.mult), inc="sDVE")
    S.op("vector", lambda e: e.tensor_tensor(R5[:, :], R1[:, :], R2[:, :],
                                             ALU.add), inc="sDVE")
    S.op("vector", lambda e: e.tensor_copy(R3[:, :], R5[:, :]), inc="sDVE")
    for it in range(4):
        S.op("vector", lambda e: e.reduce_max(
            V(R4, 0, [[1, 16]]), V(R3, 0, [[16, 16], [1, 16]]), AX.X),
            inc="sDVE")
        if it < 3:
            S.op("vector", lambda e: e.tensor_tensor(
                R2[:, :], V(R3, 0, [[16, 16], [1, 16]]),
                V(R4, 0, [[1, 16], [0, 16]]), ALU.is_ge), inc="sDVE")
            S.op("vector", lambda e: e.scalar_tensor_tensor(
                R3[:, :], R2[:, :], -BIG, R3[:, :],
                op0=ALU.mult, op1=ALU.add), inc="sDVE")
    S.op("vector", lambda e: e.tensor_tensor(
        R2[:, :], V(R5, 0, [[16, 16], [1, 16]]),
        V(R4, 0, [[1, 16], [0, 16]]), ALU.is_ge), inc="sDVE")
    S.op("vector", lambda e: e.tensor_tensor(R3[:, :], scoreA[:, :], R2[:, :],
                                             ALU.mult), inc="sDVE")
    S.op("vector", lambda e: e.reduce_sum(
        V(R4, 16, [[1, 16]]), V(R3, 0, [[16, 16], [1, 16]]), AX.X), inc="sDVE")
    S.op("vector", lambda e: e.tensor_scalar(
        V(R4, 32, [[1, 16]]), V(R4, 16, [[1, 16]]), 1e-20, None, op0=ALU.add),
        inc="sDVE")
    S.op("vector", lambda e: e.reciprocal(
        V(R4, 32, [[1, 16]]), V(R4, 32, [[1, 16]])), inc="sDVE")
    S.op("vector", lambda e: e.tensor_tensor(
        R1[:, :], R3[:, :], V(R4, 32, [[1, 16], [0, 16]]), ALU.mult),
        inc="sDVE")
    dve_c = S.cnt["sDVE"]
    return dve_t, act_t, act_s, dve_c


def build_P4(S, gpool, seg, sgw, suw, sdw, wnat, wgT, wuT, wdT, ssg,
             lnf, x_res, ident, pb):
    lnf_ready = S.now("sDVE", "sACT")
    for cb in range(4):
        for wi, (wdr, dstT, iscol) in enumerate(
                ((sgw, wgT, False), (suw, wuT, False), (sdw, wdT, True))):
            for c4 in range(4):
                ch = cb * 4 + c4
                buf = wnat if (wi * 4 + c4) % 2 == 0 else ssg
                st = gpool.stage()
                if not iscol:
                    src = dap(wdr, ch * 128 * DM, [[DM, 128], [1, DM]])
                else:
                    src = dap(wdr, ch * 128,
                              [[HSH, 128], [128 * HSH, 8], [1, 128]])
                st.dma(buf[:, 0:1024], src, waits=S.now("sPE"))
                wwt = st.wait()
                for half in range(2):
                    for j in range(4):
                        jj = half * 4 + j
                        S.op("tensor", lambda e, buf=buf, jj=jj, j=j:
                             e.transpose(
                                 pb[7][:, j * 128:j * 128 + 128],
                                 buf[:, jj * 128:jj * 128 + 128], ident[:, :]),
                             waits=[wwt] + S.now("sDVE", "sACT"), inc="sPE")
                    pe_t = S.cnt["sPE"]
                    dst = dstT[:, c4 * 1024 + half * 512:
                               c4 * 1024 + half * 512 + 512]
                    if half == 0:
                        S.op("vector", lambda e, dst=dst: e.tensor_copy(
                            dst, pb[7][:, 0:512]),
                            waits=[("sPE", pe_t)], inc="sDVE")
                    else:
                        S.op("scalar", lambda e, dst=dst: e.copy(
                            dst, pb[7][:, 0:512]),
                            waits=[("sPE", pe_t)], inc="sACT")
        wT_ready = S.now("sDVE", "sACT")
        silu_act = [0, 0]
        multu_dve = 0
        for tg in range(8):
            pend = None
            for c4 in range(4):
                par = c4 % 2
                for kc in range(KC):
                    S.op("tensor", lambda e, c4=c4, kc=kc, tg=tg, par=par:
                         e.matmul(
                             pb[par][:, 0:256],
                             wgT[:, c4 * 1024 + kc * 128:
                                 c4 * 1024 + kc * 128 + 128],
                             lnf[:, kc * SEG + tg * 256:
                                 kc * SEG + tg * 256 + 256],
                             start=(kc == 0), stop=(kc == KC - 1)),
                         waits=lnf_ready + wT_ready +
                         [("sACT", silu_act[par])], inc="sPE")
                for kc in range(KC):
                    S.op("tensor", lambda e, c4=c4, kc=kc, tg=tg:
                         e.matmul(
                             pb[2][:, 0:256],
                             wuT[:, c4 * 1024 + kc * 128:
                                 c4 * 1024 + kc * 128 + 128],
                             lnf[:, kc * SEG + tg * 256:
                                 kc * SEG + tg * 256 + 256],
                             start=(kc == 0), stop=(kc == KC - 1)),
                         waits=[("sDVE", multu_dve)], inc="sPE")
                pe_gu = S.cnt["sPE"]
                if pend is not None:
                    c4p, dve_p = pend
                    for t2 in range(2):
                        for fs in range(2):
                            ob = 3 + t2 * 2 + fs
                            S.op("tensor", lambda e, c4p=c4p, t2=t2, fs=fs,
                                 ob=ob: e.matmul(
                                     pb[ob][:, 0:512],
                                     ssg[:, c4p * 256 + t2 * 128:
                                         c4p * 256 + t2 * 128 + 128],
                                     wdT[:, c4p * 1024 + fs * 512:
                                         c4p * 1024 + fs * 512 + 512],
                                     start=(c4p == 0), stop=False),
                                 waits=[("sDVE", dve_p)], inc="sPE")
                S.op("scalar", lambda e, c4=c4, par=par:
                     e.activation(ssg[:, c4 * 256:c4 * 256 + 256],
                                  pb[par][:, 0:256], AF.Silu),
                     waits=[("sPE", pe_gu)], inc="sACT")
                silu_act[par] = S.cnt["sACT"]
                S.op("vector", lambda e, c4=c4:
                     e.tensor_tensor(ssg[:, c4 * 256:c4 * 256 + 256],
                                     ssg[:, c4 * 256:c4 * 256 + 256],
                                     pb[2][:, 0:256], ALU.mult),
                     waits=[("sACT", silu_act[par])], inc="sDVE")
                multu_dve = S.cnt["sDVE"]
                pend = (c4, multu_dve)
            c4p, dve_p = pend
            for t2 in range(2):
                for fs in range(2):
                    ob = 3 + t2 * 2 + fs
                    S.op("tensor", lambda e, c4p=c4p, t2=t2, fs=fs, ob=ob:
                         e.matmul(pb[ob][:, 0:512],
                                  ssg[:, c4p * 256 + t2 * 128:
                                      c4p * 256 + t2 * 128 + 128],
                                  wdT[:, c4p * 1024 + fs * 512:
                                      c4p * 1024 + fs * 512 + 512],
                                  start=False, stop=True),
                         waits=[("sDVE", dve_p)], inc="sPE")
            pe_dn = S.cnt["sPE"]
            for t2 in range(2):
                for fs in range(2):
                    ob = 3 + t2 * 2 + fs
                    col = (tg * 2 + t2) * 1024 + fs * 512
                    S.op("vector", lambda e, ob=ob, col=col: e.tensor_tensor(
                        x_res[:, col:col + 512], x_res[:, col:col + 512],
                        pb[ob][:, 0:512], ALU.add),
                        waits=[("sPE", pe_dn)], inc="sDVE")


def build_P5(S, gpool, seg, gw, uw, dw, sgw, suw, sdw, weg, weu, wed, hw_sb,
             R1, lnf, lnf_td, act_s, dve_c, rt_guard, x_res, ident, pb):
    dve_t, act_t = lnf_td
    NSH = HSH // EH
    NEI = NSH + NE

    estage = {}
    gu_pe = {}
    dn_pe = {}

    def load_expert(ei, prev_gu, prev_dn, extra=()):
        par = ei % 2
        st = gpool.stage()
        if ei < NSH:
            sg = dap(sgw, ei * EH, [[HSH, 128], [128 * HSH, 8], [1, EH]])
            su = dap(suw, ei * EH, [[HSH, 128], [128 * HSH, 8], [1, EH]])
            sd = dap(sdw, ei * EH * DM, [[DM, 128], [128 * DM, 2], [1, DM]])
        else:
            e_ = ei - NSH
            sg = dap(gw, e_ * DM * EH, [[EH, 128], [128 * EH, 8], [1, EH]])
            su = dap(uw, e_ * DM * EH, [[EH, 128], [128 * EH, 8], [1, EH]])
            sd = dap(dw, e_ * EH * DM, [[DM, 128], [128 * DM, 2], [1, DM]])
        st.dma(weg[par][:, :], sg, waits=list(extra) + [("sPE", prev_gu)])
        st.dma(weu[par][:, :], su, waits=[("sPE", prev_gu)])
        st.dma(wed[par][:, :], sd, waits=[("sPE", prev_dn)])
        estage[ei] = st

    load_expert(0, 0, 0, extra=rt_guard)
    load_expert(1, 0, 0)

    units = [(e_, G) for e_ in range(NEI) for G in range(4)]
    silu_prev = 0
    mult_prev = 0
    hw_free = [0, 0]
    bank_free = [0, 0]
    pend = None
    g_done = [0] * NT

    def emit_down(p):
        uidx, par_u, par_e, e_, dve_m = p
        ee_, G_ = units[uidx]
        for t2 in range(4):
            bp = t2 % 2
            for fs in range(2):
                ob = 4 + bp * 2 + fs
                for hc in range(2):
                    S.op("tensor", lambda e, hc=hc, t2=t2, fs=fs, ob=ob,
                         par_u=par_u, par_e=par_e: e.matmul(
                             pb[ob][:, 0:512],
                             hw_sb[:, par_u * 1024 + hc * 512 + t2 * 128:
                                   par_u * 1024 + hc * 512 + t2 * 128 + 128],
                             wed[par_e][:, hc * 1024 + fs * 512:
                                        hc * 1024 + fs * 512 + 512],
                             start=(hc == 0), stop=(hc == 1)),
                         waits=[("sDVE", dve_m),
                                ("sDVE", bank_free[bp])], inc="sPE")
            pe_dn = S.cnt["sPE"]
            T = G_ * 4 + t2
            for fs in range(2):
                ob = 4 + bp * 2 + fs
                col = T * 1024 + fs * 512
                if e_ < NSH:
                    S.op("vector", lambda e, ob=ob, col=col: e.tensor_tensor(
                        x_res[:, col:col + 512], x_res[:, col:col + 512],
                        pb[ob][:, 0:512], ALU.add),
                        waits=[("sPE", pe_dn)], inc="sDVE")
                else:
                    er = e_ - NSH
                    S.op("vector", lambda e, ob=ob, col=col, T=T, er=er:
                         e.scalar_tensor_tensor(
                             x_res[:, col:col + 512], pb[ob][:, 0:512],
                             R1[:, T * 16 + er:T * 16 + er + 1],
                             x_res[:, col:col + 512],
                             op0=ALU.mult, op1=ALU.add),
                         waits=[("sPE", pe_dn), ("sDVE", dve_c)],
                         inc="sDVE")
            bank_free[bp] = S.cnt["sDVE"]
            if e_ == NEI - 1:
                g_done[G_ * 4 + t2] = S.cnt["sDVE"]
        dn_pe[e_] = S.cnt["sPE"]
        hw_free[par_u] = S.cnt["sPE"]

    for uidx, (e_, G) in enumerate(units):
        par_u = uidx % 2
        par_e = e_ % 2
        ww = estage[e_].wait()
        lnw = ([("sDVE", dve_t[G * 4 + 3]), ("sACT", act_t[G * 4 + 3])]
               if e_ == 0 else [])
        for hc in range(2):
            for kc in range(KC):
                S.op("tensor", lambda e, hc=hc, kc=kc, G=G, par_e=par_e:
                     e.matmul(
                         pb[hc][:, 0:512],
                         weg[par_e][:, kc * 256 + hc * 128:
                                    kc * 256 + hc * 128 + 128],
                         lnf[:, kc * SEG + G * 512:kc * SEG + G * 512 + 512],
                         start=(kc == 0), stop=(kc == KC - 1)),
                     waits=[ww] + lnw +
                     [("sACT", silu_prev)], inc="sPE")
        pe_g = S.cnt["sPE"]
        uw_ = [("sACT", act_s)] if e_ == 0 else []
        for hc in range(2):
            for kc in range(KC):
                S.op("tensor", lambda e, hc=hc, kc=kc, G=G, par_e=par_e:
                     e.matmul(
                         pb[2 + hc][:, 0:512],
                         weu[par_e][:, kc * 256 + hc * 128:
                                    kc * 256 + hc * 128 + 128],
                         lnf[:, kc * SEG + G * 512:kc * SEG + G * 512 + 512],
                         start=(kc == 0), stop=(kc == KC - 1)),
                     waits=uw_ + [("sDVE", mult_prev)], inc="sPE")
        pe_u = S.cnt["sPE"]
        gu_pe[e_] = pe_u
        if pend is not None:
            emit_down(pend)
            ep = pend[3]
            if pend[0] % 4 == 3 and ep + 2 < NEI:
                load_expert(ep + 2, gu_pe[ep], dn_pe[ep])
        for hc in range(2):
            S.op("scalar", lambda e, hc=hc, par_u=par_u: e.activation(
                hw_sb[:, par_u * 1024 + hc * 512:
                      par_u * 1024 + hc * 512 + 512],
                pb[hc][:, 0:512], AF.Silu),
                waits=[("sPE", pe_g), ("sPE", hw_free[par_u])], inc="sACT")
        silu_prev = S.cnt["sACT"]
        for hc in range(2):
            S.op("vector", lambda e, hc=hc, par_u=par_u: e.tensor_tensor(
                hw_sb[:, par_u * 1024 + hc * 512:
                      par_u * 1024 + hc * 512 + 512],
                hw_sb[:, par_u * 1024 + hc * 512:
                      par_u * 1024 + hc * 512 + 512],
                pb[2 + hc][:, 0:512], ALU.mult),
                waits=[("sACT", silu_prev), ("sPE", pe_u)], inc="sDVE")
        mult_prev = S.cnt["sDVE"]
        pend = (uidx, par_u, par_e, e_, mult_prev)
    emit_down(pend)
    return g_done



def build_program():
    nc = bass.Bass("TRN2", target_bir_lowering=False, detect_race_conditions=False)

    x_in = nc.dram_tensor("x", [SEQ, DM], F32, kind="ExternalInput").ap()
    wq = nc.dram_tensor("wq", [DM, DM], F32, kind="ExternalInput").ap()
    bq = nc.dram_tensor("bq", [DM], F32, kind="ExternalInput").ap()
    wk = nc.dram_tensor("wk", [DM, DM], F32, kind="ExternalInput").ap()
    bk = nc.dram_tensor("bk", [DM], F32, kind="ExternalInput").ap()
    wv = nc.dram_tensor("wv", [DM, DM], F32, kind="ExternalInput").ap()
    bv = nc.dram_tensor("bv", [DM], F32, kind="ExternalInput").ap()
    rw = nc.dram_tensor("rw", [NE, DM], F32, kind="ExternalInput").ap()
    cbias = nc.dram_tensor("cbias", [NE], F32, kind="ExternalInput").ap()
    gw = nc.dram_tensor("gw", [NE, DM, EH], F32, kind="ExternalInput").ap()
    uw = nc.dram_tensor("uw", [NE, DM, EH], F32, kind="ExternalInput").ap()
    dw = nc.dram_tensor("dw", [NE, EH, DM], F32, kind="ExternalInput").ap()
    sgw = nc.dram_tensor("sgw", [DM, HSH], F32, kind="ExternalInput").ap()
    suw = nc.dram_tensor("suw", [DM, HSH], F32, kind="ExternalInput").ap()
    sdw = nc.dram_tensor("sdw", [HSH, DM], F32, kind="ExternalInput").ap()
    c_ident = nc.dram_tensor("c_ident", [128, 128], F32, kind="ExternalInput").ap()
    c_mask0 = nc.dram_tensor("c_mask0", [128, 128], F32, kind="ExternalInput").ap()
    c_msks = nc.dram_tensor("c_msks", [128, 128], F32, kind="ExternalInput").ap()
    c_poolA = nc.dram_tensor("c_poolA", [128, 128], F32, kind="ExternalInput").ap()
    c_poolB = nc.dram_tensor("c_poolB", [128, 128], F32, kind="ExternalInput").ap()
    out_t = nc.dram_tensor("out", [SEQ, DM], F32, kind="ExternalOutput").ap()
    taps = {}
    if DEBUG_TAPS:
        for nm, shp in TAP_LIST or (
                ("tap_ln1", [128, 16384]), ("tap_xa", [128, 16384]),
                        ("tap_ln2", [128, 16384]), ("tap_score", [128, 256]),
                        ("tap_cfm", [16, 2048]), ("tap_xs", [128, 16384]),
                        ("tap_q", [128, 3072]), ("tap_v", [128, 3120]),
                        ("tap_yac", [65, 4096])):
            taps[nm] = nc.dram_tensor(nm, shp, F32, kind="ExternalOutput").ap()

    sems = ["sPE", "sDVE", "sACT", "sPOOL", "sOUT", "sTAP",
            "sA0", "sA1", "sA2", "sB0", "sB1", "sB2"]
    S = Sched(nc, sems)
    spool = DmaPool(S, "sync", ["sA0", "sA1", "sA2"])
    gpool = DmaPool(S, "gpsimd", ["sB0", "sB1", "sB2"])

    stack = contextlib.ExitStack()

    with stack:
        for s_ in sems:
            S.sems[s_] = stack.enter_context(nc.semaphore(s_))

        def sb(name, shape):
            return stack.enter_context(nc.sbuf_tensor(name, shape, F32))

        def psg(name):
            return stack.enter_context(nc.psum_tensor(name, [128, 512], F32))

        x_res = sb("x_res", [128, NT * 1024])
        lnf = sb("lnf", [128, KC * SEG])
        ident = sb("ident", [128, 128])
        mask0 = sb("mask0", [128, 128])
        msks = sb("msks", [128, 128])
        poolA = sb("poolA", [128, 128])
        poolB = sb("poolB", [128, 128])
        bq_c = sb("bq_c", [128, 8])
        bk_c = sb("bk_c", [128, 8])
        bv_c = sb("bv_c", [128, 8])
        rwT = sb("rwT", [128, 128])
        cb_bc = sb("cb_bc", [128, 16])
        cb_row = sb("cb_row", [1, 16])

        pb = [psg(f"pb{i}") for i in range(8)]

        with nc.sbuf_tensor("rnat", [16, 1024], F32) as rnat:
            st0 = spool.stage()
            st0.dma(ident[:, :], c_ident)
            st0.dma(mask0[:, :], c_mask0)
            st0.dma(msks[:, :], c_msks)
            st0.dma(poolA[:, :], c_poolA)
            st0.dma(poolB[:, :], c_poolB)
            st0.dma(rnat[0:16, :], rw)
            st0.dma(cb_row[0:1, :], dap(cbias, 0, [[16, 1], [1, 16]]))
            st1 = gpool.stage()
            st1.dma(bq_c[:, :], dap(bq, 0, [[1, 128], [128, 8]]), slow=True)
            st1.dma(bk_c[:, :], dap(bk, 0, [[1, 128], [128, 8]]), slow=True)
            st1.dma(bv_c[:, :], dap(bv, 0, [[1, 128], [128, 8]]), slow=True)
            w0 = st0.wait()
            w1 = st1.wait()

            for kc in range(KC):
                S.op("tensor", lambda e, kc=kc: e.transpose(
                    pb[5][:, kc * 16:kc * 16 + 16],
                    rnat[0:16, kc * 128:kc * 128 + 128], ident[0:16, 0:16]),
                    waits=[w0], inc="sPE")
            lhs_ones = sap(ident, 0, [[128, 1], [0, 128]])
            S.op("tensor", lambda e: e.matmul(
                pb[6][:, 0:16], lhs_ones, cb_row[0:1, :], start=True, stop=True),
                waits=[w0, w1], inc="sPE")
            tpre = S.cnt["sPE"]
            S.op("vector", lambda e: e.tensor_copy(rwT[:, :], pb[5][:, 0:128]),
                 waits=[("sPE", tpre)], inc="sDVE")
            S.op("vector", lambda e: e.tensor_copy(cb_bc[:, :], pb[6][:, 0:16]),
                 waits=[("sPE", tpre)], inc="sDVE")
            S.op("scalar", lambda e: e.activation(
                cb_row[0:1, 0:1], ident[0:1, 0:1], AF.Exp),
                waits=[("sPE", tpre)], inc="sACT")
            S.emit_block()

        for seg in range(NSEG):
            with contextlib.ExitStack() as att_scope:
                def sba(name, shape):
                    return att_scope.enter_context(
                        nc.sbuf_tensor(f"{name}_s{seg}", shape, F32))

                qT = sba("qT", [128, 3072])
                kT = sba("kT", [128, 3072])
                vT = sba("vT", [128, 3120])
                Yac = sba("Yac", [128, 4096])
                nm_sb = sba("nm_sb", [128, 32])
                wq_sl = sba("wq_sl", [128, 1024])
                wk_sl = sba("wk_sl", [128, 1024])
                wv_sl = sba("wv_sl", [128, 1024])

                xst = [spool.stage(), gpool.stage(),
                       spool.stage(), gpool.stage()]
                for q in range(4):
                    src = dap(x_in, (seg * SEG + q * 512) * DM,
                              [[DM, 128], [128 * DM, 4], [1, DM]])
                    ow = max(0, 64 * (seg - 1) + 16 * (q + 1)) if seg else 0
                    xst[q].dma(x_res[:, q * 4096:q * 4096 + 4096], src,
                               waits=[("sOUT", ow)] + S.now("sDVE", "sPE"))
                xw = [st.wait() for st in xst]
                for e in ENGINES:
                    S.fence_engine(e)
                with nc.sbuf_tensor(f"ln_tm1_s{seg}", [128, 1024], F32) as ln_tm, \
                     nc.sbuf_tensor(f"stat1_s{seg}", [128, 16], F32) as stat:
                    ln_tiles(S, x_res, lnf, ln_tm, stat, ident, pb,
                             lambda t: [xw[t // 4]])
                    S.emit_block()
                S_sb = sba("S_sb", [128, 1536])
                A_sb = sba("A_sb", [128, 1536])

                def tap(nm, src):
                    if DEBUG_TAPS and seg == 0 and nm in taps:
                        S.op("sync", lambda e, d=taps[nm], s=src:
                             e.dma_start(d, s),
                             waits=S.now("sDVE", "sACT", "sPE", "sPOOL"),
                             inc="sTAP", n=16)
                tap("tap_ln1", lnf[:, :])

                if not SKIP_P2:
                    build_P2(S, spool, seg, wq, wk, wv, wq_sl, wk_sl, wv_sl,
                             lnf, qT, kT, vT, Yac, S_sb, A_sb, nm_sb,
                             x_res, ident, mask0, msks, poolA, poolB,
                             bq_c, bk_c, bv_c, pb)
                tap("tap_xa", x_res[:, :])
                tap("tap_q", qT[:, :])
                tap("tap_v", vT[:, :])
                tap("tap_yac", Yac[0:65, :])
                S.emit_block()

            with contextlib.ExitStack() as ffn_scope:
                def sbf(name, shape, prt=128):
                    return ffn_scope.enter_context(
                        nc.sbuf_tensor(f"{name}_s{seg}", [prt, shape], F32))

                scoreA = sbf("scoreA", 256)
                R1 = sbf("R1", 256)
                R2 = sbf("R2", 256)
                R3 = sbf("R3", 256)
                R4 = sbf("R4", 256)
                R5 = sbf("R5", 256)
                ln_tm2 = sbf("ln_tm2", 1024)
                stat2 = sbf("stat2", 16)
                weg = [sbf(f"weg{i}", 2048) for i in range(2)]
                weu = [sbf(f"weu{i}", 2048) for i in range(2)]
                wed = [sbf(f"wed{i}", 2048) for i in range(2)]
                hw_sb = sbf("hw", 2048)

                for e in ENGINES:
                    S.fence_engine(e)
                rt_guard = S.now("sPE", "sDVE", "sACT")
                if not SKIP_P3:
                    p3r = build_P3(S, seg, x_res, lnf, ln_tm2, stat2, ident,
                                   rwT, cb_bc, scoreA, R1, R2, R3, R4, R5, pb)
                else:
                    p3r = ([0] * NT, [0] * NT, 0, 0)
                dve_t2, act_t2, act_s, dve_c = p3r
                tap("tap_ln2", lnf[:, :])
                tap("tap_score", scoreA[:, :])
                if not SKIP_P5:
                    g_done = build_P5(S, gpool, seg, gw, uw, dw, sgw, suw,
                                      sdw, weg, weu, wed, hw_sb, R1, lnf,
                                      (dve_t2, act_t2), act_s, dve_c,
                                      rt_guard, x_res, ident, pb)
                else:
                    g_done = [S.cnt["sDVE"]] * NT
                for G in range(4):
                    dst = dap(out_t, (seg * SEG + G * 512) * DM,
                              [[DM, 128], [128 * DM, 4], [1, DM]])
                    S.op("sync", lambda e, dst=dst, G=G: e.dma_start(
                        dst, x_res[:, G * 4096:G * 4096 + 4096]),
                        waits=[("sDVE", g_done[G * 4 + 3])],
                        inc="sOUT", n=16)
                S.emit_block()

        S.op("sync", None, waits=[("sOUT", S.cnt["sOUT"]),
                                  ("sTAP", S.cnt["sTAP"])])
        S.emit_block()

    return nc


def make_consts():
    ident = np.eye(128, dtype=np.float32)
    bi = np.arange(128) // 16
    mask0 = np.where(bi[:, None] == bi[None, :], 0.0, MASKNEG).astype(np.float32)
    msks = np.where((bi[:, None] ^ 1) == bi[None, :], 0.0, MASKNEG).astype(np.float32)
    poolA = np.zeros((128, 128), dtype=np.float32)
    poolB = np.zeros((128, 128), dtype=np.float32)
    for m in range(64):
        poolA[2 * m, m] = 1.0
        poolA[2 * m + 1, m] = 1.0
        poolB[2 * m, 64 + m] = 1.0
        poolB[2 * m + 1, 64 + m] = 1.0
    return ident, mask0, msks, poolA, poolB


def kernel(L0_attn_hidden_states, L0_attn_Wq, L0_attn_bq, L0_attn_Wk,
           L0_attn_bk, L0_attn_Wv, L0_attn_bv, L0_ffn_router_w,
           L0_ffn_correction_bias, L0_ffn_gate_w, L0_ffn_up_w, L0_ffn_down_w,
           L0_ffn_shared_gate_w, L0_ffn_shared_up_w, L0_ffn_shared_down_w):
    x = np.ascontiguousarray(np.asarray(L0_attn_hidden_states, dtype=np.float32))
    ident, mask0, msks, poolA, poolB = make_consts()

    def c32(a):
        return np.ascontiguousarray(np.asarray(a, dtype=np.float32))

    common = {
        "wq": c32(L0_attn_Wq), "bq": c32(L0_attn_bq),
        "wk": c32(L0_attn_Wk), "bk": c32(L0_attn_bk),
        "wv": c32(L0_attn_Wv), "bv": c32(L0_attn_bv),
        "rw": c32(L0_ffn_router_w), "cbias": c32(L0_ffn_correction_bias),
        "gw": c32(L0_ffn_gate_w), "uw": c32(L0_ffn_up_w),
        "dw": c32(L0_ffn_down_w),
        "sgw": c32(np.asarray(L0_ffn_shared_gate_w).T),
        "suw": c32(np.asarray(L0_ffn_shared_up_w).T),
        "sdw": c32(np.asarray(L0_ffn_shared_down_w).T),
        "c_ident": ident, "c_mask0": mask0, "c_msks": msks,
        "c_poolA": poolA, "c_poolB": poolB,
    }
    nc = build_program()
    in_maps = [dict(common, x=np.ascontiguousarray(x[c])) for c in range(NCORE)]
    res = run_bass_kernel_spmd(nc, in_maps, core_ids=list(range(NCORE)))
    out = np.stack([np.asarray(r["out"], dtype=np.float32)
                    for r in res.results], axis=0)
    return out
